# version 20
# baseline (speedup 1.0000x reference)
import contextlib
import numpy as np
import concourse.bass as bass
import concourse.mybir as mybir
from concourse.bass_utils import run_bass_kernel_spmd

F32 = mybir.dt.float32
BF16 = mybir.dt.bfloat16
AF = mybir.ActivationFunctionType
OP = mybir.AluOpType
AX = mybir.AxisListType.X

Bq, Sq, D, H, HD = 4, 4096, 1024, 16, 64
T = 2048
SC = 512
NSC = T // SC
CH = 128
DI = 2816
ITILE = 256
EPS_A = 1e-6
EPS_LN = 1e-5
C12 = float(np.float32(1e-12))
C16 = float(1.0 / 6.0)

ENGINES = ("sync", "gpsimd", "vector", "scalar", "tensor")
STAGE = 3
SUB = 4


class Prog:
    def __init__(self, nc):
        self.nc = nc
        self.ops = {e: [] for e in ENGINES}
        self.sems = {}
        self.cnt = {}
        self.es = contextlib.ExitStack()

    def sem(self, name):
        self.sems[name] = self.es.enter_context(self.nc.semaphore(name))
        self.cnt[name] = 0

    def sb(self, name, shape, dt=F32):
        return self.es.enter_context(self.nc.sbuf_tensor(name, shape, dt))

    def ps(self, name):
        return self.es.enter_context(self.nc.psum_tensor(name, [128, 512], F32))

    def emit(self, eng, fn, waits=(), inc=None, inc_n=1):
        waits = tuple((s, v) for s, v in waits if v > 0)
        sems = self.sems

        def op(e, fn=fn, waits=waits, inc=inc, inc_n=inc_n):
            for s, v in waits:
                e.wait_ge(sems[s], v)
            r = fn(e)
            if inc is not None:
                r.then_inc(sems[inc], inc_n)

        self.ops[eng].append(op)
        if inc is not None:
            self.cnt[inc] += inc_n
            return self.cnt[inc]
        return None

    def t(self, sem):
        return self.cnt[sem]


def build_program():
    nc = bass.Bass("TRN2", target_bir_lowering=False)
    p = Prog(nc)

    dx = nc.dram_tensor("x_own", [T, D], F32, kind="ExternalInput").ap()
    dxp = nc.dram_tensor("x_pre", [T, D], F32, kind="ExternalInput").ap()
    dflag = nc.dram_tensor("flag", [128, 1], F32, kind="ExternalInput").ap()
    dwq = nc.dram_tensor("wq", [D, D], F32, kind="ExternalInput").ap()
    dwk = nc.dram_tensor("wk", [D, D], F32, kind="ExternalInput").ap()
    dwv = nc.dram_tensor("wv", [D, D], F32, kind="ExternalInput").ap()
    dwo = nc.dram_tensor("wo", [D, D], F32, kind="ExternalInput").ap()
    dwg = nc.dram_tensor("w_gate", [DI, D], F32, kind="ExternalInput").ap()
    dwu = nc.dram_tensor("w_up", [DI, D], F32, kind="ExternalInput").ap()
    dwd = nc.dram_tensor("w_down", [D, DI], F32, kind="ExternalInput").ap()
    dout = nc.dram_tensor("out", [T, D], F32, kind="ExternalOutput").ap()
    DBG = globals().get("DBGMODE", 0)
    if DBG:
        ddq = nc.dram_tensor("dbg_q", [128, 4096], F32, kind="ExternalOutput").ap()
        ddk = nc.dram_tensor("dbg_k", [128, 4096], F32, kind="ExternalOutput").ap()
        ddv = nc.dram_tensor("dbg_v", [128, 4096], F32, kind="ExternalOutput").ap()
        dda = nc.dram_tensor("dbg_ao", [128, 4096], F32, kind="ExternalOutput").ap()
        dds = nc.dram_tensor("dbg_s", [128, 520], F32, kind="ExternalOutput").ap()

    dx_v = dx.rearrange("(c p) f -> p c f", p=128)
    dxp_v = dxp.rearrange("(c p) f -> p c f", p=128)
    dout_v = dout.rearrange("(c p) f -> p c f", p=128)

    X = p.sb("X", [128, 16 * D])
    Xv = X[:].rearrange("p (c f) -> p c f", f=D)
    flag = p.sb("flagb", [128, 1])
    mask = p.sb("mask", [128, 128])
    ident = p.sb("ident", [128, 128])
    omeps = p.sb("omeps", [128, 1])
    LNm = p.sb("LNm", [128, 1])
    LNs = p.sb("LNs", [128, 1])
    LNv = p.sb("LNv", [128, 1])
    SCR = p.sb("SCR", [128, 1024])
    SCR2 = p.sb("SCR2", [128, 1024])
    DEN = p.sb("DEN", [128, 16])
    RDEN = p.sb("RDEN", [128, 16])
    FL = p.sb("FL", [128, 16])

    ARENA = p.sb("ARENA", [128, 34240])
    AB = ARENA[:]

    def cv32(off_b, n):
        assert off_b % 32 == 0 and off_b + 4 * n <= 136960
        return AB[:, off_b // 4: off_b // 4 + n]

    def cv16(off_b, n):
        assert off_b % 32 == 0 and n % 2 == 0 and off_b + 2 * n <= 136960
        return AB[:, off_b // 4: off_b // 4 + n // 2].bitcast(BF16)

    NW = 2
    wraw = [cv32(0, 2048), cv32(8192, 2048)]
    whi = [cv16(16384, 2048), cv16(20480, 2048)]
    wlo = [cv16(24576, 2048), cv16(28672, 2048)]
    Xf_hi = cv16(32768, 4096)
    Xf_lo = cv16(40960, 4096)
    Qf = cv32(49152, 4096)
    Kf = cv32(65536, 4096)
    Vf = cv32(81920, 4096)
    Ktm = cv32(98304, 4096)
    Vex = cv32(114688, 4160)
    AT = cv32(131328, 128)
    S_ext = cv32(131840, 520)
    Sv = S_ext.rearrange("p (j e) -> p j e", e=65)
    AO = Vf
    PW_hi = Xf_hi
    PW_lo = Xf_lo
    NUM = SCR2[:, 0:1024]

    XQ_hi = cv16(0, 16384)
    XQ_lo = cv16(32768, 16384)
    QS_t = cv32(65536, 64)
    QS_s = cv32(65792, 64)
    QS_r = cv32(66048, 64)
    QS_xn = cv32(66304, 1024)
    QS_a = cv32(70400, 1024)
    QS_b = cv32(74496, 1024)
    QS_sg = cv32(78592, 1024)
    QS_q = cv32(82688, 1024)
    QS_q2 = cv32(86784, 1024)
    WGU_hi = cv16(90880, 8 * 2 * ITILE)
    WGU_lo = cv16(99072, 8 * 2 * ITILE)
    WD_hi = cv16(107264, 2 * D)
    WD_lo = cv16(111360, 2 * D)
    FW = cv32(115456, 2 * D)
    HT = cv32(131840, ITILE)
    HF_hi = cv16(133888, 2 * CH)
    HF_lo = cv16(134400, 2 * CH)
    SILU = cv32(134912, ITILE)

    P = [p.ps(f"P{i}") for i in range(8)]

    for s in ("ld", "st", "ve", "sc", "gp", "te"):
        p.sem(s)

    def vdrain(hard=False):
        if hard:
            p.emit("vector", lambda e: e.drain())
        else:
            p.emit("vector", lambda e: e.tensor_copy(FL[:, 8:16], FL[:, 0:8]))

    pb_freed = {i: [] for i in range(8)}

    ld_x = p.emit("sync", lambda e: e.dma_start(out=Xv, in_=dx_v), inc="ld", inc_n=16)
    ld_fl = p.emit("sync", lambda e: e.dma_start(out=flag[:], in_=dflag),
                   inc="ld", inc_n=16)

    def init_const(e):
        e.memset(mask[:], 1.0)
        e.affine_select(mask[:], mask[:], [[1, 128]], OP.is_ge, 0.0,
                        base=0, channel_multiplier=-1)
        e.memset(ident[:], 1.0)
        e.affine_select(ident[:], ident[:], [[-1, 128]], OP.is_equal, 0.0,
                        base=0, channel_multiplier=1)
        e.memset(S_ext[:], 0.0)
        return e.memset(
            Vex[:].rearrange("p (c h e) -> p c h e", h=16, e=65)[:, :, :, 64:65], 1.0)
    gp_const = p.emit("gpsimd", init_const, inc="gp")

    def init_omeps(e):
        e.memset(FL[:], 0.0)
        e.memset(omeps[:], 1.0)
        return e.tensor_single_scalar(omeps[:], omeps[:], float(2.0 ** -46),
                                      OP.subtract)
    p.emit("vector", init_omeps, inc="ve")
    vdrain()

    def transpose_x_to_fm(xsrc_v, base_tile, deps):
        xh = Xf_hi[:].rearrange("p (g t) -> p g t", t=SC)
        xl = Xf_lo[:].rearrange("p (g t) -> p g t", t=SC)
        first = list(deps)
        for tc in range(4):
            ctile = base_tile + tc
            for fc in range(8):
                bk = 6 + (tc * 8 + fc) % 2
                bank = P[bk]
                te_v = p.emit(
                    "tensor",
                    lambda e, ct=ctile, f=fc, b=bank: e.transpose(
                        b[:, 0:128], xsrc_v[:, ct, f * 128:(f + 1) * 128], ident[:]),
                    waits=first + pb_freed[bk] + [("gp", gp_const)],
                    inc="te")
                first = []
                hi_v = p.emit(
                    "vector",
                    lambda e, f=fc, tc=tc, b=bank: e.tensor_copy(
                        xh[:, f, tc * 128:(tc + 1) * 128], b[:, 0:128]),
                    waits=[("te", te_v)], inc="ve")
                lo_v = p.emit(
                    "vector",
                    lambda e, f=fc, tc=tc, b=bank: e.tensor_tensor(
                        xl[:, f, tc * 128:(tc + 1) * 128], b[:, 0:128],
                        xh[:, f, tc * 128:(tc + 1) * 128], OP.subtract),
                    inc="ve")
                pb_freed[bk] = [("ve", lo_v)]
        return [("ve", p.t("ve"))]

    def stream_w_echunk(dw, off, buf_i, deps):
        src = dw[:, off:off + 256].rearrange("(g p) e -> p g e", p=128)
        dst = wraw[buf_i][:].rearrange("p (g e) -> p g e", e=256)
        ldv = p.emit("sync", lambda e, s=src, d=dst: e.dma_start(out=d, in_=s),
                     waits=list(deps), inc="ld", inc_n=16)
        hi_v = p.emit("scalar",
                      lambda e, i=buf_i: e.copy(whi[i][:], wraw[i][:]),
                      waits=[("ld", ldv), ("te", p.t("te"))], inc="sc")
        lo_v = p.emit("gpsimd",
                      lambda e, i=buf_i: e.tensor_sub(wlo[i][:], wraw[i][:], whi[i][:]),
                      waits=[("sc", hi_v)], inc="gp")
        return [("gp", lo_v), ("sc", hi_v)]

    def project_qkv(include_q, xf_deps):
        outs = ([(Qf, dwq)] if include_q else []) + [(Kf, dwk), (Vf, dwv)]
        xh = Xf_hi[:].rearrange("p (g t) -> p g t", t=SC)
        xl = Xf_lo[:].rearrange("p (g t) -> p g t", t=SC)
        wdeps = {0: [("te", p.t("te"))], 1: [("te", p.t("te"))]}
        bi = 0
        for dstbuf, dw in outs:
            dstv = dstbuf[:].rearrange("p (g t) -> p g t", t=SC)
            for ec in range(4):
                wd = stream_w_echunk(dw, ec * 256, bi, wdeps[bi])
                for sub in range(2):
                    ecg = ec * 2 + sub
                    bank = P[ecg % 2]

                    def mmfn(e, bi=bi, sub=sub, b=bank):
                        wh = whi[bi][:].rearrange("p (g e) -> p g e", e=256)
                        wl = wlo[bi][:].rearrange("p (g e) -> p g e", e=256)
                        r = None
                        for f in range(8):
                            sth = wh[:, f, sub * 128:(sub + 1) * 128]
                            stl = wl[:, f, sub * 128:(sub + 1) * 128]
                            e.matmul(b[:], sth, xh[:, f, :], start=(f == 0),
                                     stop=False)
                            e.matmul(b[:], sth, xl[:, f, :], start=False, stop=False)
                            r = e.matmul(b[:], stl, xh[:, f, :], start=False,
                                         stop=(f == 7))
                        return r
                    te_v = p.emit("tensor", mmfn,
                                  waits=wd + xf_deps + pb_freed[ecg % 2],
                                  inc="te")
                    dst = dstv[:, ecg, :]
                    if ecg % 2 == 0:
                        ev = p.emit("vector",
                                    lambda e, d=dst, b=bank: e.tensor_copy(
                                        d, b[:]),
                                    waits=[("te", te_v)], inc="ve")
                        pb_freed[0] = [("ve", ev)]
                    else:
                        ev = p.emit("scalar",
                                    lambda e, d=dst, b=bank: e.copy(d, b[:]),
                                    waits=[("te", te_v)], inc="sc")
                        pb_freed[1] = [("sc", ev)]
                wdeps[bi] = [("te", p.t("te"))]
                bi ^= 1
        return [("ve", p.t("ve")), ("sc", p.t("sc"))]

    def feature_map_qk(include_q, deps):
        bufs = ([Qf] if include_q else []) + [Kf]
        ex_prev = 0
        first = list(deps)
        for buf in bufs:
            for q4 in range(4):
                sl0, sl1 = q4 * 1024, (q4 + 1) * 1024
                mn = p.emit("vector",
                            lambda e, b=buf, s0=sl0, s1=sl1: e.tensor_scalar_min(
                                SCR[:], b[:, s0:s1], 0.0),
                            waits=first + ([("sc", ex_prev)] if ex_prev else []),
                            inc="ve")
                first = []
                vdrain()
                ex_prev = p.emit("scalar",
                                 lambda e: e.activation(SCR2[:], SCR[:], AF.Exp),
                                 waits=[("ve", mn)], inc="sc")
                p.emit("vector",
                       lambda e, b=buf, s0=sl0, s1=sl1: e.scalar_tensor_tensor(
                           b[:, s0:s1], b[:, s0:s1], 0.0, SCR2[:], OP.max, OP.add),
                       waits=[("sc", ex_prev)], inc="ve")
        return [("ve", p.t("ve"))]

    def transpose_kv(deps):
        vex = Vex[:].rearrange("p (c h e) -> p c h e", h=16, e=65)
        ktm = Ktm[:].rearrange("p (c f) -> p c f", f=D)
        kf = Kf[:].rearrange("p (g t) -> p g t", t=SC)
        vf = Vf[:].rearrange("p (g t) -> p g t", t=SC)
        first = list(deps)
        n = 0
        for tc in range(4):
            for fc in range(8):
                for which in range(2):
                    bk = 6 + n % 2
                    bank = P[bk]
                    n += 1
                    srcf = (kf if which == 0 else vf)[:, fc, tc * 128:(tc + 1) * 128]
                    te_v = p.emit("tensor",
                                  lambda e, s=srcf, b=bank: e.transpose(
                                      b[:, 0:128], s, ident[:]),
                                  waits=first + pb_freed[bk] + [("gp", gp_const)],
                                  inc="te")
                    first = []
                    if which == 0:
                        ev = p.emit("vector",
                                    lambda e, f=fc, t=tc, b=bank: e.tensor_copy(
                                        ktm[:, t, f * 128:(f + 1) * 128],
                                        b[:, 0:128]),
                                    waits=[("te", te_v)], inc="ve")
                    else:
                        ev = p.emit("vector",
                                    lambda e, f=fc, t=tc, b=bank: e.tensor_copy(
                                        vex[:, t, 2 * f:2 * f + 2, 0:64],
                                        b[:, 0:128].rearrange(
                                            "p (h d) -> p h d", d=64)),
                                    waits=[("te", te_v)], inc="ve")
                    pb_freed[bk] = [("ve", ev)]
        return [("ve", p.t("ve"))]

    def state_update(tc_range, kv_deps):
        ktm = Ktm[:].rearrange("p (c f) -> p c f", f=D)
        vex = Vex[:].rearrange("p (c h e) -> p c h e", h=16, e=65)
        first = list(kv_deps)
        for tc in tc_range:
            for h in range(16):
                fg, off = h // 2, (h % 2) * 64
                bk = 2 + h % 2
                bank = P[bk]
                te_v = p.emit(
                    "tensor",
                    lambda e, tc=tc, fg=fg, off=off, h=h, b=bank: e.matmul(
                        b[off:off + 64, 0:65],
                        ktm[:, tc, fg * 128 + off:fg * 128 + off + 64],
                        vex[:, tc, h, :], start=True, stop=True),
                    waits=first + pb_freed[bk], inc="te")
                first = []
                add = p.emit("vector",
                             lambda e, fg=fg, off=off, b=bank: e.tensor_add(
                                 Sv[off:off + 64, fg, :],
                                 Sv[off:off + 64, fg, :],
                                 b[off:off + 64, 0:65]),
                             waits=[("te", te_v)], inc="ve")
                pb_freed[bk] = [("ve", add)]
        return [("ve", p.t("ve"))]

    def attention_chunks(kv_deps, q_deps):
        vex = Vex[:].rearrange("p (c h e) -> p c h e", h=16, e=65)
        qf = Qf[:].rearrange("p (g t) -> p g t", t=SC)
        kf = Kf[:].rearrange("p (g t) -> p g t", t=SC)
        ao = AO[:].rearrange("p (c f) -> p c f", f=D)
        numv = NUM[:].rearrange("p (h d) -> p h d", d=64)
        ATb = [AT[:], SCR[:, 0:128]]
        first = list(kv_deps) + list(q_deps)

        def emit_at(tc, h, extra):
            fg, off = h // 2, (h % 2) * 64
            bk = h % 2

            def at_fn(e, fg=fg, off=off, tc=tc, b=P[bk]):
                return e.matmul(
                    b[:, 0:128],
                    kf[:, fg, tc * 128:(tc + 1) * 128][off:off + 64, :],
                    qf[:, fg, tc * 128:(tc + 1) * 128][off:off + 64, :],
                    start=True, stop=True)
            return p.emit("tensor", at_fn,
                          waits=extra + pb_freed[bk] + [("gp", gp_const)],
                          inc="te")

        for tc in range(4):
            te_at_next = emit_at(tc, 0, first)
            first = []
            for h in range(16):
                fg, off = h // 2, (h % 2) * 64
                abk = h % 2
                nd_bank = P[4 if h % 2 == 0 else 5]
                nd_off = ((h // 2) % 4) * 128
                te_at = te_at_next
                if h + 1 < 16:
                    te_at_next = emit_at(tc, h + 1, [])
                ve_m = p.emit("vector",
                              lambda e, b=P[abk], d=ATb[h % 2]:
                              e.scalar_tensor_tensor(
                                  d, b[:, 0:128], 1.0, mask[:],
                                  OP.mult, OP.mult),
                              waits=[("te", te_at), ("gp", gp_const)],
                              inc="ve")
                pb_freed[abk] = [("ve", ve_m)]

                def nd_fn(e, h=h, fg=fg, off=off, tc=tc, b=nd_bank, no=nd_off,
                          at_s=ATb[h % 2]):
                    sl = b[:, no:no + 65]
                    e.matmul(sl, at_s, vex[:, tc, h, :], start=True, stop=False)
                    return e.matmul(
                        sl, qf[:, fg, tc * 128:(tc + 1) * 128][off:off + 64, :],
                        Sv[off:off + 64, fg, :], start=False, stop=True)
                te_nd = p.emit("tensor", nd_fn, waits=[("ve", ve_m)], inc="te")
                nv = p.emit("vector",
                            lambda e, h=h, b=nd_bank, no=nd_off: e.tensor_copy(
                                numv[:, h, :], b[:, no:no + 64]),
                            waits=[("te", te_nd)], inc="ve")
                dv = p.emit("vector",
                            lambda e, h=h, b=nd_bank, no=nd_off: e.tensor_copy(
                                DEN[:, h:h + 1], b[:, no + 64:no + 65]),
                            inc="ve")
                pb_freed[4 + abk] = [("ve", dv)]
            vdrain()
            p.emit("vector",
                   lambda e: e.tensor_single_scalar(DEN[:], DEN[:], EPS_A, OP.add),
                   inc="ve")
            vdrain()
            p.emit("vector", lambda e: e.reciprocal(RDEN[:], DEN[:]), inc="ve")
            vdrain(True)
            p.emit("vector",
                   lambda e, tc=tc: e.tensor_tensor(
                       ao[:, tc, :].rearrange("p (h d) -> p h d", d=64),
                       numv,
                       RDEN[:].unsqueeze(2).broadcast_to([128, 16, 64]), OP.mult),
                   inc="ve")
            vdrain()
            state_update([tc], [])
        return [("ve", p.t("ve"))]

    def ln_tile(xap, deps):
        p.emit("vector", lambda e: e.reduce_sum(LNm[:], xap, AX),
               waits=list(deps), inc="ve")
        vdrain()
        sq = p.emit("scalar",
                    lambda e: e.activation(SCR2[:], xap, AF.Square,
                                           accum_out=LNs[:]),
                    waits=list(deps) + [("ve", p.t("ve"))], inc="sc")
        p.emit("vector",
               lambda e: e.tensor_single_scalar(LNm[:], LNm[:], 1.0 / D, OP.mult),
               inc="ve")
        vdrain()
        p.emit("vector", lambda e: e.tensor_tensor(LNv[:], LNm[:], LNm[:], OP.mult),
               waits=[("sc", sq)], inc="ve")
        p.emit("vector",
               lambda e: e.tensor_single_scalar(SCR[:, 0:1], LNs[:], 1.0 / D,
                                                OP.mult),
               inc="ve")
        vdrain()
        p.emit("vector",
               lambda e: e.tensor_tensor(LNv[:], SCR[:, 0:1], LNv[:], OP.subtract),
               inc="ve")
        vdrain()
        p.emit("vector",
               lambda e: e.tensor_single_scalar(LNv[:], LNv[:], EPS_LN, OP.add),
               inc="ve")
        vdrain()
        rt = p.emit("scalar", lambda e: e.activation(LNs[:], LNv[:], AF.Sqrt),
                    waits=[("ve", p.t("ve"))], inc="sc")
        p.emit("vector", lambda e: e.reciprocal(LNv[:], LNs[:]),
               waits=[("sc", rt)], inc="ve")
        vdrain(True)
        v = p.emit("vector",
                   lambda e: e.tensor_scalar(xap, xap, LNm[0:128, 0:1],
                                             LNv[0:128, 0:1], OP.subtract, OP.mult),
                   inc="ve")
        vdrain()
        return v

    def wo_and_ln1(sc_idx, ao_deps):
        ao = AO[:].rearrange("p (c f) -> p c f", f=D)
        pwh = PW_hi[:].rearrange("p (g t) -> p g t", t=SC)
        pwl = PW_lo[:].rearrange("p (g t) -> p g t", t=SC)
        first = list(ao_deps)
        pw_deps = []
        for tc in range(4):
            dsth = [pwh[:, fc, tc * 128:(tc + 1) * 128] for fc in range(8)]
            dstl = [pwl[:, fc, tc * 128:(tc + 1) * 128] for fc in range(8)]
            pw_deps, _ = transpose_split(ao[:, tc, :], 8, dsth, dstl, first)
            first = []
        if SUB < 3:
            return pw_deps
        for half in range(2):
            wd0 = stream_w_echunk(dwo, half * 512, 0, pw_deps)
            wd1 = stream_w_echunk(dwo, half * 512 + 256, 1, pw_deps)
            for tc in range(4):
                ctile = sc_idx * 4 + tc
                bank = P[2 if tc % 2 == 0 else 3]

                def mmfn(e, tc=tc, b=bank):
                    whs = [whi[0][:].rearrange("p (g e) -> p g e", e=256),
                           whi[1][:].rearrange("p (g e) -> p g e", e=256)]
                    wls = [wlo[0][:].rearrange("p (g e) -> p g e", e=256),
                           wlo[1][:].rearrange("p (g e) -> p g e", e=256)]
                    r = None
                    for piece in range(2):
                        bsl = b[:, piece * 256:piece * 256 + 256]
                        for g in range(8):
                            sth = pwh[:, g, tc * 128:(tc + 1) * 128]
                            stl = pwl[:, g, tc * 128:(tc + 1) * 128]
                            e.matmul(bsl, sth, whs[piece][:, g, :],
                                     start=(g == 0), stop=False)
                            e.matmul(bsl, sth, wls[piece][:, g, :],
                                     start=False, stop=False)
                            r = e.matmul(bsl, stl, whs[piece][:, g, :],
                                         start=False, stop=(g == 7))
                    return r
                te_v = p.emit("tensor", mmfn,
                              waits=wd0 + wd1 + pw_deps + pb_freed[2 + tc % 2],
                              inc="te")
                add = p.emit("vector",
                             lambda e, ct=ctile, hf=half, b=bank: e.tensor_add(
                                 Xv[:, ct, hf * 512:(hf + 1) * 512],
                                 Xv[:, ct, hf * 512:(hf + 1) * 512], b[:]),
                             waits=[("te", te_v)], inc="ve")
                pb_freed[2 + tc % 2] = [("ve", add)]
            pw_deps = [("te", p.t("te"))]
        if SUB < 4:
            return [("ve", p.t("ve"))]
        for tc in range(4):
            ln_tile(Xv[:, sc_idx * 4 + tc, :], [])
        return [("ve", p.t("ve"))]

    def transpose_split(src, n, dsth, dstl, deps):
        first = list(deps)
        for i in range(n):
            bk = 6 + (i % 2)
            bank = P[bk]
            te_v = p.emit("tensor",
                          lambda e, i=i, b=bank: e.transpose(
                              b[:, 0:128], src[:, i * 128:(i + 1) * 128],
                              ident[:]),
                          waits=first + pb_freed[bk] + [("gp", gp_const)],
                          inc="te")
            first = []
            hi = p.emit("scalar",
                        lambda e, i=i, b=bank: e.copy(dsth[i], b[:, 0:128]),
                        waits=[("te", te_v)], inc="sc")
            lo = p.emit("vector",
                        lambda e, i=i, b=bank: e.tensor_tensor(
                            dstl[i], b[:, 0:128], dsth[i], OP.subtract),
                        waits=[("sc", hi)], inc="ve")
            pb_freed[bk] = [("sc", hi), ("ve", lo)]
        return [("ve", p.t("ve")), ("sc", p.t("sc"))], [("te", p.t("te"))]

    def quant_tile(xap, F, deps, q_waits=(), qo=None):
        Bn = F // 16
        if qo is None:
            qo = QS_q[:, 0:F]
        x3 = xap.rearrange("p (b k) -> p b k", k=16)
        p.emit("vector",
               lambda e: e.tensor_reduce(QS_t[:, 0:Bn], x3, AX, OP.max,
                                         apply_absolute_value=True),
               waits=list(deps), inc="ve")
        vdrain()
        p.emit("vector",
               lambda e: e.tensor_scalar(QS_s[:, 0:Bn], QS_t[:, 0:Bn], C16, C12,
                                         OP.mult, OP.max), inc="ve")
        vdrain()
        p.emit("vector", lambda e: e.reciprocal(QS_r[:, 0:Bn], QS_s[:, 0:Bn]),
               inc="ve")
        vdrain(True)
        p.emit("vector",
               lambda e: e.tensor_tensor(QS_r[:, 0:Bn], QS_r[:, 0:Bn],
                                         omeps[:].broadcast_to([128, Bn]),
                                         OP.mult), inc="ve")
        vdrain()
        ua = p.emit("scalar", lambda e: e.activation(QS_a[:, 0:F], xap, AF.Abs),
                    waits=list(deps) + [("ve", p.t("ve"))], inc="sc")
        sg = p.emit("scalar", lambda e: e.activation(QS_sg[:, 0:F], xap, AF.Sign),
                    inc="sc")
        p.emit("vector",
               lambda e: e.tensor_tensor(
                   QS_xn[:, 0:F].rearrange("p (b k) -> p b k", k=16),
                   QS_a[:, 0:F].rearrange("p (b k) -> p b k", k=16),
                   QS_r[:, 0:Bn].unsqueeze(2).broadcast_to([128, Bn, 16]),
                   OP.mult),
               waits=[("sc", ua)], inc="ve")
        vdrain()
        p.emit("vector",
               lambda e: e.tensor_single_scalar(QS_a[:, 0:F], QS_xn[:, 0:F],
                                                0.25, OP.is_gt), inc="ve")
        vdrain()
        for m in (0.75, 1.25, 1.75):
            p.emit("vector",
                   lambda e, m=m: e.scalar_tensor_tensor(
                       QS_a[:, 0:F], QS_xn[:, 0:F], m, QS_a[:, 0:F],
                       OP.is_gt, OP.add), inc="ve")
        p.emit("vector",
               lambda e: e.tensor_scalar(QS_b[:, 0:F], QS_xn[:, 0:F], 5.0, 2.0,
                                         OP.is_gt, OP.mult), inc="ve")
        vdrain()
        for m in (2.5, 3.5):
            p.emit("vector",
                   lambda e, m=m: e.scalar_tensor_tensor(
                       QS_b[:, 0:F], QS_xn[:, 0:F], m, QS_b[:, 0:F],
                       OP.is_gt, OP.add), inc="ve")
        cb = p.emit("vector",
                    lambda e: e.scalar_tensor_tensor(
                        QS_a[:, 0:F], QS_a[:, 0:F], 0.5, QS_b[:, 0:F],
                        OP.mult, OP.add), inc="ve")
        qs = p.emit("gpsimd",
                    lambda e: e.tensor_mul(QS_b[:, 0:F], QS_a[:, 0:F],
                                           QS_sg[:, 0:F]),
                    waits=[("ve", cb), ("sc", sg)], inc="gp")
        xq = p.emit("vector",
                    lambda e: e.tensor_tensor(
                        qo.rearrange("p (b k) -> p b k", k=16),
                        QS_b[:, 0:F].rearrange("p (b k) -> p b k", k=16),
                        QS_s[:, 0:Bn].unsqueeze(2).broadcast_to([128, Bn, 16]),
                        OP.mult),
                    waits=[("gp", qs)] + list(q_waits), inc="ve")
        return [("ve", xq)], [("sc", sg), ("ve", xq)]

    ao_v = AO[:].rearrange("p (c f) -> p c f", f=D)
    for psc in range(NSC if STAGE >= 1 else 0):
        ldp = p.emit("sync",
                     lambda e, s=psc: e.dma_start(
                         out=ao_v, in_=dxp_v[:, s * 4:(s + 1) * 4, :]),
                     waits=[("te", p.t("te")), ("ve", p.t("ve"))],
                     inc="ld", inc_n=16)
        xf = transpose_x_to_fm(ao_v, 0, [("ld", ldp)])
        pj = project_qkv(include_q=False, xf_deps=xf)
        fm = feature_map_qk(include_q=False, deps=pj)
        kv = transpose_kv(fm)
        state_update(range(4), kv)

    p.emit("vector",
           lambda e: e.tensor_tensor(S_ext[:], S_ext[:],
                                     flag[:].broadcast_to([128, 520]), OP.mult),
           waits=[("ld", ld_fl), ("ve", p.t("ve"))], inc="ve")
    vdrain()

    def dbg_dump(dst, srcap):
        p.emit("sync", lambda e, d=dst, s=srcap: e.dma_start(out=d, in_=s),
               waits=[("ve", p.t("ve")), ("sc", p.t("sc")), ("gp", p.t("gp")),
                      ("te", p.t("te"))],
               inc="st", inc_n=16)
        stv = p.t("st")
        p.emit("vector", lambda e: e.nop(), waits=[("st", stv)], inc="ve")
        p.emit("scalar", lambda e: e.nop(), waits=[("st", stv)], inc="sc")

    for sc in range(NSC if STAGE >= 2 else 0):
        xf = transpose_x_to_fm(Xv, sc * 4, [("ld", ld_x)])
        pj = project_qkv(include_q=True, xf_deps=xf)
        fm = feature_map_qk(include_q=True, deps=pj)
        kv = transpose_kv(fm)
        if DBG and sc == 0:
            dbg_dump(ddq, Qf)
            dbg_dump(ddk, Kf)
            dbg_dump(ddv, Vf)
            dbg_dump(dds, S_ext)
        if SUB >= 1:
            at = attention_chunks(kv, fm)
        else:
            at = kv
        if DBG and sc == 0 and SUB >= 1:
            dbg_dump(dda, AO)
        if SUB >= 2:
            wo_and_ln1(sc, at)

    ph1 = [(s, p.t(s)) for s in ("ld", "ve", "sc", "gp", "te")]
    xq_h = XQ_hi[:].rearrange("p (g t) -> p g t", t=T)
    xq_l = XQ_lo[:].rearrange("p (g t) -> p g t", t=T)
    qdeps = list(ph1)
    qfree = {"A0": 0, "A1": 0, "B": 0}
    QBUF = {"A": QS_q, "B": QS_q2}

    def q_guard(buf):
        if buf == "A":
            return [("te", max(qfree["A0"], qfree["A1"]))]
        return [("te", qfree[buf])]

    def q_mark(buf, tev):
        if buf == "B":
            qfree["B"] = tev
        elif buf == "A":
            qfree["A0"] = qfree["A1"] = tev
        else:
            qfree[buf] = tev

    pend_x = None
    for ct in range(16 if STAGE >= 3 else 0):
        buf = "A" if ct % 2 == 0 else "B"
        qready, _ = quant_tile(Xv[:, ct, :], D, qdeps, q_waits=q_guard(buf),
                               qo=QBUF[buf][:, 0:D])
        qdeps = []
        if pend_x is not None:
            pct, pbuf, pqr = pend_x
            dsth = [xq_h[:, f, pct * 128:(pct + 1) * 128] for f in range(8)]
            dstl = [xq_l[:, f, pct * 128:(pct + 1) * 128] for f in range(8)]
            transpose_split(QBUF[pbuf][:, 0:D], 8, dsth, dstl, pqr)
            q_mark(pbuf, p.t("te"))
        pend_x = (ct, buf, qready)
    if pend_x is not None:
        pct, pbuf, pqr = pend_x
        dsth = [xq_h[:, f, pct * 128:(pct + 1) * 128] for f in range(8)]
        dstl = [xq_l[:, f, pct * 128:(pct + 1) * 128] for f in range(8)]
        transpose_split(QBUF[pbuf][:, 0:D], 8, dsth, dstl, pqr)
        q_mark(pbuf, p.t("te"))
    xq_deps = [("ve", p.t("ve")), ("sc", p.t("sc"))]

    itiles = [256] * 11 if STAGE >= 3 else []
    i_off = 0
    fw = FW[:].rearrange("p (d f) -> p d f", f=D)
    wgu_h = WGU_hi[:].rearrange("p (g i) -> p g i", i=2 * ITILE)
    wgu_l = WGU_lo[:].rearrange("p (g i) -> p g i", i=2 * ITILE)
    wd_h = WD_hi[:].rearrange("p (c o) -> p c o", o=D)
    wd_l = WD_lo[:].rearrange("p (c o) -> p c o", o=D)
    hf_h = HF_hi[:].rearrange("p (c t) -> p c t", t=CH)
    hf_l = HF_lo[:].rearrange("p (c t) -> p c t", t=CH)
    fwbuf_deps = {0: list(ph1), 1: list(ph1)}
    GUB = [0, 1, 4]
    gu_free = {0: [], 1: [], 2: []}
    dn_free = []
    hv_prev = 0
    pend_h = []
    NIC_C, IW_C = ITILE // 128, ITILE

    def flush_h(ct, akey, qready, defer_adds=False):
        nonlocal dn_free
        dsth = [hf_h[:, i, :] for i in range(NIC_C)]
        dstl = [hf_l[:, i, :] for i in range(NIC_C)]
        hf_deps, _ = transpose_split(
            QS_q[:, (ct % 2) * IW_C:(ct % 2 + 1) * IW_C], NIC_C, dsth, dstl,
            qready)
        q_mark(akey, p.t("te"))

        def dn_fn(e):
            r = None
            for half, b2 in ((0, P[2]), (1, P[3])):
                for i in range(NIC_C):
                    wdh = wd_h[:, i, half * 512:(half + 1) * 512]
                    wdl = wd_l[:, i, half * 512:(half + 1) * 512]
                    e.matmul(b2[:], hf_h[:, i, :], wdh, start=(i == 0),
                             stop=False)
                    e.matmul(b2[:], hf_h[:, i, :], wdl, start=False,
                             stop=False)
                    r = e.matmul(b2[:], hf_l[:, i, :], wdh, start=False,
                                 stop=(i == NIC_C - 1))
            return r
        te_dn = p.emit("tensor", dn_fn, waits=hf_deps + dn_free, inc="te")
        if defer_adds:
            return (ct, te_dn)
        emit_adds(ct, te_dn)
        return None

    def emit_adds(ctp, te_dn):
        nonlocal dn_free
        last_add = None
        for half, b2 in ((0, P[2]), (1, P[3])):
            last_add = p.emit("vector",
                              lambda e, c=ctp, hf=half, b=b2: e.tensor_add(
                                  Xv[:, c, hf * 512:(hf + 1) * 512],
                                  Xv[:, c, hf * 512:(hf + 1) * 512], b[:]),
                              waits=[("te", te_dn)], inc="ve")
        dn_free = [("ve", last_add)]

    for it_idx, IW in enumerate(itiles):
        NIC = IW // 128
        for pe in pend_h:
            flush_h(*pe)
        pend_h = []
        wtiles = ([("gu", wb, dw, ic) for wb, dw in ((0, dwg), (ITILE, dwu))
                   for ic in range(NIC)] + [("wd", oc) for oc in range(8)])
        pend_w = None
        for j, t in enumerate(wtiles):
            bi = j % 2
            buf = "A" if j % 2 == 0 else "B"
            Fw_ = D if t[0] == "gu" else IW
            if t[0] == "gu":
                _, wb, dw, ic = t
                row0 = i_off + ic * 128
                ld_v = p.emit("sync",
                              lambda e, r=row0, b=bi, d=dw: e.dma_start(
                                  out=fw[:, b, :], in_=d[r:r + 128, :]),
                              waits=fwbuf_deps[bi], inc="ld", inc_n=16)
            else:
                oc = t[1]
                ld_v = p.emit("sync",
                              lambda e, o=oc, b=bi, i0=i_off, iw=IW:
                              e.dma_start(out=fw[:, b, 0:iw],
                                          in_=dwd[o * 128:(o + 1) * 128,
                                                  i0:i0 + iw]),
                              waits=fwbuf_deps[bi], inc="ld", inc_n=16)
            qready, xdone = quant_tile(fw[:, bi, 0:Fw_], Fw_, [("ld", ld_v)],
                                       q_waits=q_guard(buf),
                                       qo=QBUF[buf][:, 0:Fw_])
            fwbuf_deps[bi] = xdone
            if pend_w is not None:
                pend_w()
            if t[0] == "gu":
                dsth = [wgu_h[:, f, wb + ic * 128:wb + (ic + 1) * 128]
                        for f in range(8)]
                dstl = [wgu_l[:, f, wb + ic * 128:wb + (ic + 1) * 128]
                        for f in range(8)]
                n_t = 8
            else:
                dsth = [wd_h[:, i, oc * 128:(oc + 1) * 128] for i in range(NIC)]
                dstl = [wd_l[:, i, oc * 128:(oc + 1) * 128] for i in range(NIC)]
                n_t = NIC

            def pend_w(buf=buf, Fw_=Fw_, n_t=n_t, dsth=dsth, dstl=dstl,
                       qready=qready):
                transpose_split(QBUF[buf][:, 0:Fw_], n_t, dsth, dstl, qready)
                q_mark(buf, p.t("te"))
        pend_w()
        wq_deps = [("ve", p.t("ve")), ("sc", p.t("sc"))]

        def emit_gu(ct, bank, extra):
            def gu_fn(e, ct=ct, b=bank):
                r = None
                for f in range(8):
                    sth = xq_h[:, f, ct * 128:(ct + 1) * 128]
                    stl = xq_l[:, f, ct * 128:(ct + 1) * 128]
                    e.matmul(b[:], sth, wgu_h[:, f, :], start=(f == 0),
                             stop=False)
                    e.matmul(b[:], sth, wgu_l[:, f, :], start=False, stop=False)
                    r = e.matmul(b[:], stl, wgu_h[:, f, :], start=False,
                                 stop=(f == 7))
                return r
            return p.emit("tensor", gu_fn, waits=wq_deps + xq_deps + extra,
                          inc="te")

        te_gu_q = [emit_gu(0, P[GUB[0]], gu_free[0]),
                   emit_gu(1, P[GUB[1]], gu_free[1])]
        for ct in range(16):
            dnp = None
            if len(pend_h) == 2:
                dnp = flush_h(*pend_h.pop(0), defer_adds=True)
            bank = P[GUB[ct % 3]]
            te_gu = te_gu_q.pop(0)
            if ct + 2 < 16:
                te_gu_q.append(emit_gu(ct + 2, P[GUB[(ct + 2) % 3]],
                                       gu_free[(ct + 2) % 3]))
            sl_v = p.emit("scalar",
                          lambda e, b=bank, iw=IW: e.activation(
                              SILU[:, 0:iw], b[:, 0:iw], AF.Silu),
                          waits=[("te", te_gu)] +
                                ([("ve", hv_prev)] if hv_prev else []),
                          inc="sc")
            h_v = p.emit("vector",
                         lambda e, b=bank, iw=IW: e.tensor_tensor(
                             HT[:, 0:iw], SILU[:, 0:iw], b[:, iw:2 * iw],
                             OP.mult),
                         waits=[("sc", sl_v)], inc="ve")
            hv_prev = h_v
            gu_free[ct % 3] = [("sc", sl_v), ("ve", h_v)]
            vdrain()
            akey = "A0" if ct % 2 == 0 else "A1"
            qready, _ = quant_tile(HT[:, 0:IW], IW, [("ve", h_v)],
                                   q_waits=q_guard(akey),
                                   qo=QS_q[:, (ct % 2) * IW:(ct % 2 + 1) * IW])
            if dnp is not None:
                emit_adds(*dnp)
            pend_h.append((ct, akey, qready))
        i_off += IW
    for pe in pend_h:
        flush_h(*pe)
    pend_h = []

    ffn_deps = [("ve", p.t("ve"))]
    for ct in range(16):
        ln_tile(Xv[:, ct, :], ffn_deps)
        ffn_deps = []
        p.emit("sync",
               lambda e, c=ct: e.dma_start(out=dout_v[:, c, :], in_=Xv[:, c, :]),
               waits=[("ve", p.t("ve")), ("sc", p.t("sc"))],
               inc="st", inc_n=16)
    st_total = p.t("st")
    p.emit("sync", lambda e: e.wait_ge(p.sems["st"], st_total))

    build_program.counts = dict(p.cnt)
    build_program.nops = {e: len(v) for e, v in p.ops.items()}
    with p.es, nc.Block() as block:
        @block.sync
        def _(sync):
            for op in p.ops["sync"]:
                op(sync)

        @block.gpsimd
        def _(gpsimd):
            for op in p.ops["gpsimd"]:
                op(gpsimd)

        @block.vector
        def _(vector):
            for op in p.ops["vector"]:
                op(vector)

        @block.scalar
        def _(scalar):
            for op in p.ops["scalar"]:
                op(scalar)

        @block.tensor
        def _(tensor):
            for op in p.ops["tensor"]:
                op(tensor)

    return nc


_NC_CACHE = None


def kernel(L0_attn_x, L0_attn_wq, L0_attn_wk, L0_attn_wv, L0_attn_wo,
           L0_ffn_w_gate, L0_ffn_w_up, L0_ffn_w_down):
    global _NC_CACHE
    x = np.ascontiguousarray(np.asarray(L0_attn_x, dtype=np.float32))
    wq = np.ascontiguousarray(np.asarray(L0_attn_wq, dtype=np.float32))
    wk = np.ascontiguousarray(np.asarray(L0_attn_wk, dtype=np.float32))
    wv = np.ascontiguousarray(np.asarray(L0_attn_wv, dtype=np.float32))
    wo = np.ascontiguousarray(np.asarray(L0_attn_wo, dtype=np.float32))
    wg = np.ascontiguousarray(np.asarray(L0_ffn_w_gate, dtype=np.float32))
    wu = np.ascontiguousarray(np.asarray(L0_ffn_w_up, dtype=np.float32))
    wd = np.ascontiguousarray(np.asarray(L0_ffn_w_down, dtype=np.float32))

    if _NC_CACHE is None:
        _NC_CACHE = build_program()
    nc = _NC_CACHE

    in_maps = []
    for c in range(8):
        b, h = c // 2, c % 2
        own = x[b, h * T:(h + 1) * T, :]
        pre = x[b, 0:T, :]
        fl = np.full((128, 1), float(h), dtype=np.float32)
        in_maps.append({
            "x_own": np.ascontiguousarray(own),
            "x_pre": np.ascontiguousarray(pre),
            "flag": fl,
            "wq": wq, "wk": wk, "wv": wv, "wo": wo,
            "w_gate": wg, "w_up": wu, "w_down": wd,
        })
    res = run_bass_kernel_spmd(nc, in_maps, core_ids=list(range(8)))
    out = np.empty((Bq, Sq, D), dtype=np.float32)
    for c in range(8):
        b, h = c // 2, c % 2
        out[b, h * T:(h + 1) * T, :] = np.asarray(res.results[c]["out"],
                                                  dtype=np.float32)
    return out


# version 21
# speedup vs baseline: 1.0109x; 1.0109x over previous
import contextlib
import numpy as np
import concourse.bass as bass
import concourse.mybir as mybir
from concourse.bass_utils import run_bass_kernel_spmd

F32 = mybir.dt.float32
BF16 = mybir.dt.bfloat16
AF = mybir.ActivationFunctionType
OP = mybir.AluOpType
AX = mybir.AxisListType.X

Bq, Sq, D, H, HD = 4, 4096, 1024, 16, 64
T = 2048
SC = 512
NSC = T // SC
CH = 128
DI = 2816
ITILE = 256
EPS_A = 1e-6
EPS_LN = 1e-5
C12 = float(np.float32(1e-12))
C16 = float(1.0 / 6.0)

ENGINES = ("sync", "gpsimd", "vector", "scalar", "tensor")
STAGE = 3
SUB = 4


class Prog:
    def __init__(self, nc):
        self.nc = nc
        self.ops = {e: [] for e in ENGINES}
        self.sems = {}
        self.cnt = {}
        self.es = contextlib.ExitStack()

    def sem(self, name):
        self.sems[name] = self.es.enter_context(self.nc.semaphore(name))
        self.cnt[name] = 0

    def sb(self, name, shape, dt=F32):
        return self.es.enter_context(self.nc.sbuf_tensor(name, shape, dt))

    def ps(self, name):
        return self.es.enter_context(self.nc.psum_tensor(name, [128, 512], F32))

    def emit(self, eng, fn, waits=(), inc=None, inc_n=1):
        waits = tuple((s, v) for s, v in waits if v > 0)
        sems = self.sems

        def op(e, fn=fn, waits=waits, inc=inc, inc_n=inc_n):
            for s, v in waits:
                e.wait_ge(sems[s], v)
            r = fn(e)
            if inc is not None:
                r.then_inc(sems[inc], inc_n)

        self.ops[eng].append(op)
        if inc is not None:
            self.cnt[inc] += inc_n
            return self.cnt[inc]
        return None

    def t(self, sem):
        return self.cnt[sem]


def build_program():
    nc = bass.Bass("TRN2", target_bir_lowering=False)
    p = Prog(nc)

    dx = nc.dram_tensor("x_own", [T, D], F32, kind="ExternalInput").ap()
    dxp = nc.dram_tensor("x_pre", [T, D], F32, kind="ExternalInput").ap()
    dflag = nc.dram_tensor("flag", [128, 1], F32, kind="ExternalInput").ap()
    dwq = nc.dram_tensor("wq", [D, D], F32, kind="ExternalInput").ap()
    dwk = nc.dram_tensor("wk", [D, D], F32, kind="ExternalInput").ap()
    dwv = nc.dram_tensor("wv", [D, D], F32, kind="ExternalInput").ap()
    dwo = nc.dram_tensor("wo", [D, D], F32, kind="ExternalInput").ap()
    dwg = nc.dram_tensor("w_gate", [DI, D], F32, kind="ExternalInput").ap()
    dwu = nc.dram_tensor("w_up", [DI, D], F32, kind="ExternalInput").ap()
    dwd = nc.dram_tensor("w_down", [D, DI], F32, kind="ExternalInput").ap()
    dout = nc.dram_tensor("out", [T, D], F32, kind="ExternalOutput").ap()
    DBG = globals().get("DBGMODE", 0)
    if DBG:
        ddq = nc.dram_tensor("dbg_q", [128, 4096], F32, kind="ExternalOutput").ap()
        ddk = nc.dram_tensor("dbg_k", [128, 4096], F32, kind="ExternalOutput").ap()
        ddv = nc.dram_tensor("dbg_v", [128, 4096], F32, kind="ExternalOutput").ap()
        dda = nc.dram_tensor("dbg_ao", [128, 4096], F32, kind="ExternalOutput").ap()
        dds = nc.dram_tensor("dbg_s", [128, 520], F32, kind="ExternalOutput").ap()

    dx_v = dx.rearrange("(c p) f -> p c f", p=128)
    dxp_v = dxp.rearrange("(c p) f -> p c f", p=128)
    dout_v = dout.rearrange("(c p) f -> p c f", p=128)

    X = p.sb("X", [128, 16 * D])
    Xv = X[:].rearrange("p (c f) -> p c f", f=D)
    flag = p.sb("flagb", [128, 1])
    mask = p.sb("mask", [128, 128])
    ident = p.sb("ident", [128, 128])
    omeps = p.sb("omeps", [128, 1])
    LNm = p.sb("LNm", [128, 1])
    LNs = p.sb("LNs", [128, 1])
    LNv = p.sb("LNv", [128, 1])
    SCR = p.sb("SCR", [128, 1024])
    SCR2 = p.sb("SCR2", [128, 1024])
    DEN = p.sb("DEN", [128, 16])
    RDEN = p.sb("RDEN", [128, 16])
    FL = p.sb("FL", [128, 16])

    ARENA = p.sb("ARENA", [128, 34240])
    AB = ARENA[:]

    def cv32(off_b, n):
        assert off_b % 32 == 0 and off_b + 4 * n <= 136960
        return AB[:, off_b // 4: off_b // 4 + n]

    def cv16(off_b, n):
        assert off_b % 32 == 0 and n % 2 == 0 and off_b + 2 * n <= 136960
        return AB[:, off_b // 4: off_b // 4 + n // 2].bitcast(BF16)

    NW = 2
    wraw = [cv32(0, 2048), cv32(8192, 2048)]
    whi = [cv16(16384, 2048), cv16(20480, 2048)]
    wlo = [cv16(24576, 2048), cv16(28672, 2048)]
    Xf_hi = cv16(32768, 4096)
    Xf_lo = cv16(40960, 4096)
    Qf = cv32(49152, 4096)
    Kf = cv32(65536, 4096)
    Vf = cv32(81920, 4096)
    Ktm = cv32(98304, 4096)
    Vex = cv32(114688, 4160)
    AT = cv32(131328, 128)
    S_ext = cv32(131840, 520)
    Sv = S_ext.rearrange("p (j e) -> p j e", e=65)
    AO = Vf
    PW_hi = Xf_hi
    PW_lo = Xf_lo
    NUM = SCR2[:, 0:1024]

    XQ_hi = cv16(0, 16384)
    XQ_lo = cv16(32768, 16384)
    QS_t = cv32(65536, 64)
    QS_s = cv32(65792, 64)
    QS_r = cv32(66048, 64)
    QS_xn = cv32(66304, 1024)
    QS_a = cv32(70400, 1024)
    QS_b = cv32(74496, 1024)
    QS_sg = cv32(78592, 1024)
    QS_q = cv32(82688, 1024)
    QS_q2 = cv32(86784, 1024)
    WGU_hi = cv16(90880, 8 * 2 * ITILE)
    WGU_lo = cv16(99072, 8 * 2 * ITILE)
    WD_hi = cv16(107264, 2 * D)
    WD_lo = cv16(111360, 2 * D)
    FW = cv32(115456, 2 * D)
    HT = cv32(131840, ITILE)
    HF_hi = cv16(133888, 2 * CH)
    HF_lo = cv16(134400, 2 * CH)
    SILU = cv32(134912, ITILE)

    P = [p.ps(f"P{i}") for i in range(8)]

    for s in ("ld", "st", "ve", "sc", "gp", "te"):
        p.sem(s)

    def vdrain(hard=False):
        if hard:
            p.emit("vector", lambda e: e.drain())
        else:
            p.emit("vector", lambda e: e.tensor_copy(FL[:, 8:16], FL[:, 0:8]))

    pb_freed = {i: [] for i in range(8)}

    ld_x = p.emit("sync", lambda e: e.dma_start(out=Xv, in_=dx_v), inc="ld", inc_n=16)
    ld_fl = p.emit("sync", lambda e: e.dma_start(out=flag[:], in_=dflag),
                   inc="ld", inc_n=16)

    def init_const(e):
        e.memset(mask[:], 1.0)
        e.affine_select(mask[:], mask[:], [[1, 128]], OP.is_ge, 0.0,
                        base=0, channel_multiplier=-1)
        e.memset(ident[:], 1.0)
        e.affine_select(ident[:], ident[:], [[-1, 128]], OP.is_equal, 0.0,
                        base=0, channel_multiplier=1)
        e.memset(S_ext[:], 0.0)
        return e.memset(
            Vex[:].rearrange("p (c h e) -> p c h e", h=16, e=65)[:, :, :, 64:65], 1.0)
    gp_const = p.emit("gpsimd", init_const, inc="gp")

    def init_omeps(e):
        e.memset(FL[:], 0.0)
        e.memset(omeps[:], 1.0)
        return e.tensor_single_scalar(omeps[:], omeps[:], float(2.0 ** -46),
                                      OP.subtract)
    p.emit("vector", init_omeps, inc="ve")
    vdrain()

    def transpose_x_to_fm(xsrc_v, base_tile, deps):
        xh = Xf_hi[:].rearrange("p (g t) -> p g t", t=SC)
        xl = Xf_lo[:].rearrange("p (g t) -> p g t", t=SC)
        first = list(deps)
        for tc in range(4):
            ctile = base_tile + tc
            for fc in range(8):
                bk = 6 + (tc * 8 + fc) % 2
                bank = P[bk]
                te_v = p.emit(
                    "tensor",
                    lambda e, ct=ctile, f=fc, b=bank: e.transpose(
                        b[:, 0:128], xsrc_v[:, ct, f * 128:(f + 1) * 128], ident[:]),
                    waits=first + pb_freed[bk] + [("gp", gp_const)],
                    inc="te")
                first = []
                hi_v = p.emit(
                    "vector",
                    lambda e, f=fc, tc=tc, b=bank: e.tensor_copy(
                        xh[:, f, tc * 128:(tc + 1) * 128], b[:, 0:128]),
                    waits=[("te", te_v)], inc="ve")
                lo_v = p.emit(
                    "vector",
                    lambda e, f=fc, tc=tc, b=bank: e.tensor_tensor(
                        xl[:, f, tc * 128:(tc + 1) * 128], b[:, 0:128],
                        xh[:, f, tc * 128:(tc + 1) * 128], OP.subtract),
                    inc="ve")
                pb_freed[bk] = [("ve", lo_v)]
        return [("ve", p.t("ve"))]

    def stream_w_echunk(dw, off, buf_i, deps):
        src = dw[:, off:off + 256].rearrange("(g p) e -> p g e", p=128)
        dst = wraw[buf_i][:].rearrange("p (g e) -> p g e", e=256)
        ldv = p.emit("sync", lambda e, s=src, d=dst: e.dma_start(out=d, in_=s),
                     waits=list(deps), inc="ld", inc_n=16)
        hi_v = p.emit("scalar",
                      lambda e, i=buf_i: e.copy(whi[i][:], wraw[i][:]),
                      waits=[("ld", ldv), ("te", p.t("te"))], inc="sc")
        lo_v = p.emit("gpsimd",
                      lambda e, i=buf_i: e.tensor_sub(wlo[i][:], wraw[i][:], whi[i][:]),
                      waits=[("sc", hi_v)], inc="gp")
        return [("gp", lo_v), ("sc", hi_v)]

    pj_state = {"bi": 0}

    def project_one(dstbuf, dw, xf_deps):
        xh = Xf_hi[:].rearrange("p (g t) -> p g t", t=SC)
        xl = Xf_lo[:].rearrange("p (g t) -> p g t", t=SC)
        wdeps = {0: [("te", p.t("te"))], 1: [("te", p.t("te"))]}
        bi = pj_state["bi"]
        dstv = dstbuf[:].rearrange("p (g t) -> p g t", t=SC)
        for ec in range(4):
            wd = stream_w_echunk(dw, ec * 256, bi, wdeps[bi])
            for sub in range(2):
                ecg = ec * 2 + sub
                bank = P[ecg % 2]

                def mmfn(e, bi=bi, sub=sub, b=bank):
                    wh = whi[bi][:].rearrange("p (g e) -> p g e", e=256)
                    wl = wlo[bi][:].rearrange("p (g e) -> p g e", e=256)
                    r = None
                    for f in range(8):
                        sth = wh[:, f, sub * 128:(sub + 1) * 128]
                        stl = wl[:, f, sub * 128:(sub + 1) * 128]
                        e.matmul(b[:], sth, xh[:, f, :], start=(f == 0),
                                 stop=False)
                        e.matmul(b[:], sth, xl[:, f, :], start=False, stop=False)
                        r = e.matmul(b[:], stl, xh[:, f, :], start=False,
                                     stop=(f == 7))
                    return r
                te_v = p.emit("tensor", mmfn,
                              waits=wd + xf_deps + pb_freed[ecg % 2],
                              inc="te")
                xf_deps = []
                dst = dstv[:, ecg, :]
                if ecg % 2 == 0:
                    ev = p.emit("vector",
                                lambda e, d=dst, b=bank: e.tensor_copy(
                                    d, b[:]),
                                waits=[("te", te_v)], inc="ve")
                    pb_freed[0] = [("ve", ev)]
                else:
                    ev = p.emit("scalar",
                                lambda e, d=dst, b=bank: e.copy(d, b[:]),
                                waits=[("te", te_v)], inc="sc")
                    pb_freed[1] = [("sc", ev)]
            wdeps[bi] = [("te", p.t("te"))]
            bi ^= 1
        pj_state["bi"] = bi
        return [("ve", p.t("ve")), ("sc", p.t("sc"))]

    def fm_one(buf, deps):
        ex_prev = 0
        first = list(deps)
        for q4 in range(4):
            sl0, sl1 = q4 * 1024, (q4 + 1) * 1024
            mn = p.emit("vector",
                        lambda e, b=buf, s0=sl0, s1=sl1: e.tensor_scalar_min(
                            SCR[:], b[:, s0:s1], 0.0),
                        waits=first + ([("sc", ex_prev)] if ex_prev else []),
                        inc="ve")
            first = []
            vdrain()
            ex_prev = p.emit("scalar",
                             lambda e: e.activation(SCR2[:], SCR[:], AF.Exp),
                             waits=[("ve", mn)], inc="sc")
            p.emit("vector",
                   lambda e, b=buf, s0=sl0, s1=sl1: e.scalar_tensor_tensor(
                       b[:, s0:s1], b[:, s0:s1], 0.0, SCR2[:], OP.max, OP.add),
                   waits=[("sc", ex_prev)], inc="ve")
        return [("ve", p.t("ve"))]

    def transpose_kv(v_deps, k_deps):
        vex = Vex[:].rearrange("p (c h e) -> p c h e", h=16, e=65)
        ktm = Ktm[:].rearrange("p (c f) -> p c f", f=D)
        kf = Kf[:].rearrange("p (g t) -> p g t", t=SC)
        vf = Vf[:].rearrange("p (g t) -> p g t", t=SC)
        n = 0
        first = list(v_deps)
        for tc in range(4):
            for fc in range(8):
                bk = 6 + n % 2
                bank = P[bk]
                n += 1
                srcf = vf[:, fc, tc * 128:(tc + 1) * 128]
                te_v = p.emit("tensor",
                              lambda e, s=srcf, b=bank: e.transpose(
                                  b[:, 0:128], s, ident[:]),
                              waits=first + pb_freed[bk] + [("gp", gp_const)],
                              inc="te")
                first = []
                ev = p.emit("vector",
                            lambda e, f=fc, t=tc, b=bank: e.tensor_copy(
                                vex[:, t, 2 * f:2 * f + 2, 0:64],
                                b[:, 0:128].rearrange(
                                    "p (h d) -> p h d", d=64)),
                            waits=[("te", te_v)], inc="ve")
                pb_freed[bk] = [("ve", ev)]
        first = list(k_deps)
        for tc in range(4):
            for fc in range(8):
                bk = 6 + n % 2
                bank = P[bk]
                n += 1
                srcf = kf[:, fc, tc * 128:(tc + 1) * 128]
                te_v = p.emit("tensor",
                              lambda e, s=srcf, b=bank: e.transpose(
                                  b[:, 0:128], s, ident[:]),
                              waits=first + pb_freed[bk] + [("gp", gp_const)],
                              inc="te")
                first = []
                ev = p.emit("vector",
                            lambda e, f=fc, t=tc, b=bank: e.tensor_copy(
                                ktm[:, t, f * 128:(f + 1) * 128],
                                b[:, 0:128]),
                            waits=[("te", te_v)], inc="ve")
                pb_freed[bk] = [("ve", ev)]
        return [("ve", p.t("ve"))]

    def state_update(tc_range, kv_deps):
        ktm = Ktm[:].rearrange("p (c f) -> p c f", f=D)
        vex = Vex[:].rearrange("p (c h e) -> p c h e", h=16, e=65)
        first = list(kv_deps)
        for tc in tc_range:
            for h in range(16):
                fg, off = h // 2, (h % 2) * 64
                bk = 2 + h % 2
                bank = P[bk]
                te_v = p.emit(
                    "tensor",
                    lambda e, tc=tc, fg=fg, off=off, h=h, b=bank: e.matmul(
                        b[off:off + 64, 0:65],
                        ktm[:, tc, fg * 128 + off:fg * 128 + off + 64],
                        vex[:, tc, h, :], start=True, stop=True),
                    waits=first + pb_freed[bk], inc="te")
                first = []
                add = p.emit("vector",
                             lambda e, fg=fg, off=off, b=bank: e.tensor_add(
                                 Sv[off:off + 64, fg, :],
                                 Sv[off:off + 64, fg, :],
                                 b[off:off + 64, 0:65]),
                             waits=[("te", te_v)], inc="ve")
                pb_freed[bk] = [("ve", add)]
        return [("ve", p.t("ve"))]

    def attention_chunks(kv_deps, q_deps):
        vex = Vex[:].rearrange("p (c h e) -> p c h e", h=16, e=65)
        qf = Qf[:].rearrange("p (g t) -> p g t", t=SC)
        kf = Kf[:].rearrange("p (g t) -> p g t", t=SC)
        ao = AO[:].rearrange("p (c f) -> p c f", f=D)
        numv = NUM[:].rearrange("p (h d) -> p h d", d=64)
        ATb = [AT[:], SCR[:, 0:128]]
        first = list(kv_deps) + list(q_deps)

        def emit_at(tc, h, extra):
            fg, off = h // 2, (h % 2) * 64
            bk = h % 2

            def at_fn(e, fg=fg, off=off, tc=tc, b=P[bk]):
                return e.matmul(
                    b[:, 0:128],
                    kf[:, fg, tc * 128:(tc + 1) * 128][off:off + 64, :],
                    qf[:, fg, tc * 128:(tc + 1) * 128][off:off + 64, :],
                    start=True, stop=True)
            return p.emit("tensor", at_fn,
                          waits=extra + pb_freed[bk] + [("gp", gp_const)],
                          inc="te")

        for tc in range(4):
            te_at_next = emit_at(tc, 0, first)
            first = []
            for h in range(16):
                fg, off = h // 2, (h % 2) * 64
                abk = h % 2
                nd_bank = P[4 if h % 2 == 0 else 5]
                nd_off = ((h // 2) % 4) * 128
                te_at = te_at_next
                if h + 1 < 16:
                    te_at_next = emit_at(tc, h + 1, [])
                ve_m = p.emit("vector",
                              lambda e, b=P[abk], d=ATb[h % 2]:
                              e.scalar_tensor_tensor(
                                  d, b[:, 0:128], 1.0, mask[:],
                                  OP.mult, OP.mult),
                              waits=[("te", te_at), ("gp", gp_const)],
                              inc="ve")
                pb_freed[abk] = [("ve", ve_m)]

                def nd_fn(e, h=h, fg=fg, off=off, tc=tc, b=nd_bank, no=nd_off,
                          at_s=ATb[h % 2]):
                    sl = b[:, no:no + 65]
                    e.matmul(sl, at_s, vex[:, tc, h, :], start=True, stop=False)
                    return e.matmul(
                        sl, qf[:, fg, tc * 128:(tc + 1) * 128][off:off + 64, :],
                        Sv[off:off + 64, fg, :], start=False, stop=True)
                te_nd = p.emit("tensor", nd_fn, waits=[("ve", ve_m)], inc="te")
                nv = p.emit("vector",
                            lambda e, h=h, b=nd_bank, no=nd_off: e.tensor_copy(
                                numv[:, h, :], b[:, no:no + 64]),
                            waits=[("te", te_nd)], inc="ve")
                dv = p.emit("vector",
                            lambda e, h=h, b=nd_bank, no=nd_off: e.tensor_copy(
                                DEN[:, h:h + 1], b[:, no + 64:no + 65]),
                            inc="ve")
                pb_freed[4 + abk] = [("ve", dv)]
            vdrain()
            p.emit("vector",
                   lambda e: e.tensor_single_scalar(DEN[:], DEN[:], EPS_A, OP.add),
                   inc="ve")
            vdrain()
            p.emit("vector", lambda e: e.reciprocal(RDEN[:], DEN[:]), inc="ve")
            vdrain(True)
            p.emit("vector",
                   lambda e, tc=tc: e.tensor_tensor(
                       ao[:, tc, :].rearrange("p (h d) -> p h d", d=64),
                       numv,
                       RDEN[:].unsqueeze(2).broadcast_to([128, 16, 64]), OP.mult),
                   inc="ve")
            vdrain()
            state_update([tc], [])
        return [("ve", p.t("ve"))]

    def ln_tile(xap, deps):
        p.emit("vector", lambda e: e.reduce_sum(LNm[:], xap, AX),
               waits=list(deps), inc="ve")
        vdrain()
        sq = p.emit("scalar",
                    lambda e: e.activation(SCR2[:], xap, AF.Square,
                                           accum_out=LNs[:]),
                    waits=list(deps) + [("ve", p.t("ve"))], inc="sc")
        p.emit("vector",
               lambda e: e.tensor_single_scalar(LNm[:], LNm[:], 1.0 / D, OP.mult),
               inc="ve")
        vdrain()
        p.emit("vector", lambda e: e.tensor_tensor(LNv[:], LNm[:], LNm[:], OP.mult),
               waits=[("sc", sq)], inc="ve")
        p.emit("vector",
               lambda e: e.tensor_single_scalar(SCR[:, 0:1], LNs[:], 1.0 / D,
                                                OP.mult),
               inc="ve")
        vdrain()
        p.emit("vector",
               lambda e: e.tensor_tensor(LNv[:], SCR[:, 0:1], LNv[:], OP.subtract),
               inc="ve")
        vdrain()
        p.emit("vector",
               lambda e: e.tensor_single_scalar(LNv[:], LNv[:], EPS_LN, OP.add),
               inc="ve")
        vdrain()
        rt = p.emit("scalar", lambda e: e.activation(LNs[:], LNv[:], AF.Sqrt),
                    waits=[("ve", p.t("ve"))], inc="sc")
        p.emit("vector", lambda e: e.reciprocal(LNv[:], LNs[:]),
               waits=[("sc", rt)], inc="ve")
        vdrain(True)
        v = p.emit("vector",
                   lambda e: e.tensor_scalar(xap, xap, LNm[0:128, 0:1],
                                             LNv[0:128, 0:1], OP.subtract, OP.mult),
                   inc="ve")
        vdrain()
        return v

    def wo_and_ln1(sc_idx, ao_deps):
        ao = AO[:].rearrange("p (c f) -> p c f", f=D)
        pwh = PW_hi[:].rearrange("p (g t) -> p g t", t=SC)
        pwl = PW_lo[:].rearrange("p (g t) -> p g t", t=SC)
        first = list(ao_deps)
        pw_deps = []
        for tc in range(4):
            dsth = [pwh[:, fc, tc * 128:(tc + 1) * 128] for fc in range(8)]
            dstl = [pwl[:, fc, tc * 128:(tc + 1) * 128] for fc in range(8)]
            pw_deps, _ = transpose_split(ao[:, tc, :], 8, dsth, dstl, first)
            first = []
        if SUB < 3:
            return pw_deps
        for half in range(2):
            wd0 = stream_w_echunk(dwo, half * 512, 0, pw_deps)
            wd1 = stream_w_echunk(dwo, half * 512 + 256, 1, pw_deps)
            for tc in range(4):
                ctile = sc_idx * 4 + tc
                bank = P[2 if tc % 2 == 0 else 3]

                def mmfn(e, tc=tc, b=bank):
                    whs = [whi[0][:].rearrange("p (g e) -> p g e", e=256),
                           whi[1][:].rearrange("p (g e) -> p g e", e=256)]
                    wls = [wlo[0][:].rearrange("p (g e) -> p g e", e=256),
                           wlo[1][:].rearrange("p (g e) -> p g e", e=256)]
                    r = None
                    for piece in range(2):
                        bsl = b[:, piece * 256:piece * 256 + 256]
                        for g in range(8):
                            sth = pwh[:, g, tc * 128:(tc + 1) * 128]
                            stl = pwl[:, g, tc * 128:(tc + 1) * 128]
                            e.matmul(bsl, sth, whs[piece][:, g, :],
                                     start=(g == 0), stop=False)
                            e.matmul(bsl, sth, wls[piece][:, g, :],
                                     start=False, stop=False)
                            r = e.matmul(bsl, stl, whs[piece][:, g, :],
                                         start=False, stop=(g == 7))
                    return r
                te_v = p.emit("tensor", mmfn,
                              waits=wd0 + wd1 + pw_deps + pb_freed[2 + tc % 2],
                              inc="te")
                add = p.emit("vector",
                             lambda e, ct=ctile, hf=half, b=bank: e.tensor_add(
                                 Xv[:, ct, hf * 512:(hf + 1) * 512],
                                 Xv[:, ct, hf * 512:(hf + 1) * 512], b[:]),
                             waits=[("te", te_v)], inc="ve")
                pb_freed[2 + tc % 2] = [("ve", add)]
            pw_deps = [("te", p.t("te"))]
        if SUB < 4:
            return [("ve", p.t("ve"))]
        for tc in range(4):
            ln_tile(Xv[:, sc_idx * 4 + tc, :], [])
        return [("ve", p.t("ve"))]

    def transpose_split(src, n, dsth, dstl, deps):
        first = list(deps)
        for i in range(n):
            bk = 6 + (i % 2)
            bank = P[bk]
            te_v = p.emit("tensor",
                          lambda e, i=i, b=bank: e.transpose(
                              b[:, 0:128], src[:, i * 128:(i + 1) * 128],
                              ident[:]),
                          waits=first + pb_freed[bk] + [("gp", gp_const)],
                          inc="te")
            first = []
            hi = p.emit("scalar",
                        lambda e, i=i, b=bank: e.copy(dsth[i], b[:, 0:128]),
                        waits=[("te", te_v)], inc="sc")
            lo = p.emit("vector",
                        lambda e, i=i, b=bank: e.tensor_tensor(
                            dstl[i], b[:, 0:128], dsth[i], OP.subtract),
                        waits=[("sc", hi)], inc="ve")
            pb_freed[bk] = [("sc", hi), ("ve", lo)]
        return [("ve", p.t("ve")), ("sc", p.t("sc"))], [("te", p.t("te"))]

    def quant_tile(xap, F, deps, q_waits=(), qo=None):
        Bn = F // 16
        if qo is None:
            qo = QS_q[:, 0:F]
        x3 = xap.rearrange("p (b k) -> p b k", k=16)
        p.emit("vector",
               lambda e: e.tensor_reduce(QS_t[:, 0:Bn], x3, AX, OP.max,
                                         apply_absolute_value=True),
               waits=list(deps), inc="ve")
        vdrain()
        p.emit("vector",
               lambda e: e.tensor_scalar(QS_s[:, 0:Bn], QS_t[:, 0:Bn], C16, C12,
                                         OP.mult, OP.max), inc="ve")
        vdrain()
        p.emit("vector", lambda e: e.reciprocal(QS_r[:, 0:Bn], QS_s[:, 0:Bn]),
               inc="ve")
        vdrain(True)
        p.emit("vector",
               lambda e: e.tensor_tensor(QS_r[:, 0:Bn], QS_r[:, 0:Bn],
                                         omeps[:].broadcast_to([128, Bn]),
                                         OP.mult), inc="ve")
        vdrain()
        ua = p.emit("scalar", lambda e: e.activation(QS_a[:, 0:F], xap, AF.Abs),
                    waits=list(deps) + [("ve", p.t("ve"))], inc="sc")
        sg = p.emit("scalar", lambda e: e.activation(QS_sg[:, 0:F], xap, AF.Sign),
                    inc="sc")
        p.emit("vector",
               lambda e: e.tensor_tensor(
                   QS_xn[:, 0:F].rearrange("p (b k) -> p b k", k=16),
                   QS_a[:, 0:F].rearrange("p (b k) -> p b k", k=16),
                   QS_r[:, 0:Bn].unsqueeze(2).broadcast_to([128, Bn, 16]),
                   OP.mult),
               waits=[("sc", ua)], inc="ve")
        vdrain()
        p.emit("vector",
               lambda e: e.tensor_single_scalar(QS_a[:, 0:F], QS_xn[:, 0:F],
                                                0.25, OP.is_gt), inc="ve")
        vdrain()
        for m in (0.75, 1.25, 1.75):
            p.emit("vector",
                   lambda e, m=m: e.scalar_tensor_tensor(
                       QS_a[:, 0:F], QS_xn[:, 0:F], m, QS_a[:, 0:F],
                       OP.is_gt, OP.add), inc="ve")
        p.emit("vector",
               lambda e: e.tensor_scalar(QS_b[:, 0:F], QS_xn[:, 0:F], 5.0, 2.0,
                                         OP.is_gt, OP.mult), inc="ve")
        vdrain()
        for m in (2.5, 3.5):
            p.emit("vector",
                   lambda e, m=m: e.scalar_tensor_tensor(
                       QS_b[:, 0:F], QS_xn[:, 0:F], m, QS_b[:, 0:F],
                       OP.is_gt, OP.add), inc="ve")
        cb = p.emit("vector",
                    lambda e: e.scalar_tensor_tensor(
                        QS_a[:, 0:F], QS_a[:, 0:F], 0.5, QS_b[:, 0:F],
                        OP.mult, OP.add), inc="ve")
        qs = p.emit("gpsimd",
                    lambda e: e.tensor_mul(QS_b[:, 0:F], QS_a[:, 0:F],
                                           QS_sg[:, 0:F]),
                    waits=[("ve", cb), ("sc", sg)], inc="gp")
        xq = p.emit("vector",
                    lambda e: e.tensor_tensor(
                        qo.rearrange("p (b k) -> p b k", k=16),
                        QS_b[:, 0:F].rearrange("p (b k) -> p b k", k=16),
                        QS_s[:, 0:Bn].unsqueeze(2).broadcast_to([128, Bn, 16]),
                        OP.mult),
                    waits=[("gp", qs)] + list(q_waits), inc="ve")
        return [("ve", xq)], [("sc", sg), ("ve", xq)]

    ao_v = AO[:].rearrange("p (c f) -> p c f", f=D)
    for psc in range(NSC if STAGE >= 1 else 0):
        ldp = p.emit("sync",
                     lambda e, s=psc: e.dma_start(
                         out=ao_v, in_=dxp_v[:, s * 4:(s + 1) * 4, :]),
                     waits=[("te", p.t("te")), ("ve", p.t("ve"))],
                     inc="ld", inc_n=16)
        xf = transpose_x_to_fm(ao_v, 0, [("ld", ldp)])
        kd = project_one(Kf, dwk, xf)
        fmk = fm_one(Kf, kd)
        vd = project_one(Vf, dwv, [])
        kv = transpose_kv(vd, fmk)
        state_update(range(4), kv)

    p.emit("vector",
           lambda e: e.tensor_tensor(S_ext[:], S_ext[:],
                                     flag[:].broadcast_to([128, 520]), OP.mult),
           waits=[("ld", ld_fl), ("ve", p.t("ve"))], inc="ve")
    vdrain()

    def dbg_dump(dst, srcap):
        p.emit("sync", lambda e, d=dst, s=srcap: e.dma_start(out=d, in_=s),
               waits=[("ve", p.t("ve")), ("sc", p.t("sc")), ("gp", p.t("gp")),
                      ("te", p.t("te"))],
               inc="st", inc_n=16)
        stv = p.t("st")
        p.emit("vector", lambda e: e.nop(), waits=[("st", stv)], inc="ve")
        p.emit("scalar", lambda e: e.nop(), waits=[("st", stv)], inc="sc")

    for sc in range(NSC if STAGE >= 2 else 0):
        xf = transpose_x_to_fm(Xv, sc * 4, [("ld", ld_x)])
        qd = project_one(Qf, dwq, xf)
        fmq = fm_one(Qf, qd)
        kd = project_one(Kf, dwk, [])
        fmk = fm_one(Kf, kd)
        vd = project_one(Vf, dwv, [])
        fm = [("ve", p.t("ve"))]
        kv = transpose_kv(vd, fmk)
        if DBG and sc == 0:
            dbg_dump(ddq, Qf)
            dbg_dump(ddk, Kf)
            dbg_dump(ddv, Vf)
            dbg_dump(dds, S_ext)
        if SUB >= 1:
            at = attention_chunks(kv, fm)
        else:
            at = kv
        if DBG and sc == 0 and SUB >= 1:
            dbg_dump(dda, AO)
        if SUB >= 2:
            wo_and_ln1(sc, at)

    ph1 = [(s, p.t(s)) for s in ("ld", "ve", "sc", "gp", "te")]
    xq_h = XQ_hi[:].rearrange("p (g t) -> p g t", t=T)
    xq_l = XQ_lo[:].rearrange("p (g t) -> p g t", t=T)
    qdeps = list(ph1)
    qfree = {"A0": 0, "A1": 0, "B": 0}
    QBUF = {"A": QS_q, "B": QS_q2}

    def q_guard(buf):
        if buf == "A":
            return [("te", max(qfree["A0"], qfree["A1"]))]
        return [("te", qfree[buf])]

    def q_mark(buf, tev):
        if buf == "B":
            qfree["B"] = tev
        elif buf == "A":
            qfree["A0"] = qfree["A1"] = tev
        else:
            qfree[buf] = tev

    pend_x = None
    for ct in range(16 if STAGE >= 3 else 0):
        buf = "A" if ct % 2 == 0 else "B"
        qready, _ = quant_tile(Xv[:, ct, :], D, qdeps, q_waits=q_guard(buf),
                               qo=QBUF[buf][:, 0:D])
        qdeps = []
        if pend_x is not None:
            pct, pbuf, pqr = pend_x
            dsth = [xq_h[:, f, pct * 128:(pct + 1) * 128] for f in range(8)]
            dstl = [xq_l[:, f, pct * 128:(pct + 1) * 128] for f in range(8)]
            transpose_split(QBUF[pbuf][:, 0:D], 8, dsth, dstl, pqr)
            q_mark(pbuf, p.t("te"))
        pend_x = (ct, buf, qready)
    if pend_x is not None:
        pct, pbuf, pqr = pend_x
        dsth = [xq_h[:, f, pct * 128:(pct + 1) * 128] for f in range(8)]
        dstl = [xq_l[:, f, pct * 128:(pct + 1) * 128] for f in range(8)]
        transpose_split(QBUF[pbuf][:, 0:D], 8, dsth, dstl, pqr)
        q_mark(pbuf, p.t("te"))
    xq_deps = [("ve", p.t("ve")), ("sc", p.t("sc"))]

    itiles = [256] * 11 if STAGE >= 3 else []
    i_off = 0
    fw = FW[:].rearrange("p (d f) -> p d f", f=D)
    wgu_h = WGU_hi[:].rearrange("p (g i) -> p g i", i=2 * ITILE)
    wgu_l = WGU_lo[:].rearrange("p (g i) -> p g i", i=2 * ITILE)
    wd_h = WD_hi[:].rearrange("p (c o) -> p c o", o=D)
    wd_l = WD_lo[:].rearrange("p (c o) -> p c o", o=D)
    hf_h = HF_hi[:].rearrange("p (c t) -> p c t", t=CH)
    hf_l = HF_lo[:].rearrange("p (c t) -> p c t", t=CH)
    fwbuf_deps = {0: list(ph1), 1: list(ph1)}
    GUB = [0, 1, 4]
    gu_free = {0: [], 1: [], 2: []}
    dn_free = []
    hv_prev = 0
    pend_h = []
    NIC_C, IW_C = ITILE // 128, ITILE

    def flush_h(ct, akey, qready, defer_adds=False):
        nonlocal dn_free
        dsth = [hf_h[:, i, :] for i in range(NIC_C)]
        dstl = [hf_l[:, i, :] for i in range(NIC_C)]
        hf_deps, _ = transpose_split(
            QS_q[:, (ct % 2) * IW_C:(ct % 2 + 1) * IW_C], NIC_C, dsth, dstl,
            qready)
        q_mark(akey, p.t("te"))

        def dn_fn(e):
            r = None
            for half, b2 in ((0, P[2]), (1, P[3])):
                for i in range(NIC_C):
                    wdh = wd_h[:, i, half * 512:(half + 1) * 512]
                    wdl = wd_l[:, i, half * 512:(half + 1) * 512]
                    e.matmul(b2[:], hf_h[:, i, :], wdh, start=(i == 0),
                             stop=False)
                    e.matmul(b2[:], hf_h[:, i, :], wdl, start=False,
                             stop=False)
                    r = e.matmul(b2[:], hf_l[:, i, :], wdh, start=False,
                                 stop=(i == NIC_C - 1))
            return r
        te_dn = p.emit("tensor", dn_fn, waits=hf_deps + dn_free, inc="te")
        if defer_adds:
            return (ct, te_dn)
        emit_adds(ct, te_dn)
        return None

    def emit_adds(ctp, te_dn):
        nonlocal dn_free
        last_add = None
        for half, b2 in ((0, P[2]), (1, P[3])):
            last_add = p.emit("vector",
                              lambda e, c=ctp, hf=half, b=b2: e.tensor_add(
                                  Xv[:, c, hf * 512:(hf + 1) * 512],
                                  Xv[:, c, hf * 512:(hf + 1) * 512], b[:]),
                              waits=[("te", te_dn)], inc="ve")
        dn_free = [("ve", last_add)]

    for it_idx, IW in enumerate(itiles):
        NIC = IW // 128
        for pe in pend_h:
            flush_h(*pe)
        pend_h = []
        wtiles = ([("gu", wb, dw, ic) for wb, dw in ((0, dwg), (ITILE, dwu))
                   for ic in range(NIC)] + [("wd", oc) for oc in range(8)])
        pend_w = None
        for j, t in enumerate(wtiles):
            bi = j % 2
            buf = "A" if j % 2 == 0 else "B"
            Fw_ = D if t[0] == "gu" else IW
            if t[0] == "gu":
                _, wb, dw, ic = t
                row0 = i_off + ic * 128
                ld_v = p.emit("sync",
                              lambda e, r=row0, b=bi, d=dw: e.dma_start(
                                  out=fw[:, b, :], in_=d[r:r + 128, :]),
                              waits=fwbuf_deps[bi], inc="ld", inc_n=16)
            else:
                oc = t[1]
                ld_v = p.emit("sync",
                              lambda e, o=oc, b=bi, i0=i_off, iw=IW:
                              e.dma_start(out=fw[:, b, 0:iw],
                                          in_=dwd[o * 128:(o + 1) * 128,
                                                  i0:i0 + iw]),
                              waits=fwbuf_deps[bi], inc="ld", inc_n=16)
            qready, xdone = quant_tile(fw[:, bi, 0:Fw_], Fw_, [("ld", ld_v)],
                                       q_waits=q_guard(buf),
                                       qo=QBUF[buf][:, 0:Fw_])
            fwbuf_deps[bi] = xdone
            if pend_w is not None:
                pend_w()
            if t[0] == "gu":
                dsth = [wgu_h[:, f, wb + ic * 128:wb + (ic + 1) * 128]
                        for f in range(8)]
                dstl = [wgu_l[:, f, wb + ic * 128:wb + (ic + 1) * 128]
                        for f in range(8)]
                n_t = 8
            else:
                dsth = [wd_h[:, i, oc * 128:(oc + 1) * 128] for i in range(NIC)]
                dstl = [wd_l[:, i, oc * 128:(oc + 1) * 128] for i in range(NIC)]
                n_t = NIC

            def pend_w(buf=buf, Fw_=Fw_, n_t=n_t, dsth=dsth, dstl=dstl,
                       qready=qready):
                transpose_split(QBUF[buf][:, 0:Fw_], n_t, dsth, dstl, qready)
                q_mark(buf, p.t("te"))
        pend_w()
        wq_deps = [("ve", p.t("ve")), ("sc", p.t("sc"))]

        def emit_gu(ct, bank, extra):
            def gu_fn(e, ct=ct, b=bank):
                r = None
                for f in range(8):
                    sth = xq_h[:, f, ct * 128:(ct + 1) * 128]
                    stl = xq_l[:, f, ct * 128:(ct + 1) * 128]
                    e.matmul(b[:], sth, wgu_h[:, f, :], start=(f == 0),
                             stop=False)
                    e.matmul(b[:], sth, wgu_l[:, f, :], start=False, stop=False)
                    r = e.matmul(b[:], stl, wgu_h[:, f, :], start=False,
                                 stop=(f == 7))
                return r
            return p.emit("tensor", gu_fn, waits=wq_deps + xq_deps + extra,
                          inc="te")

        te_gu_q = [emit_gu(0, P[GUB[0]], gu_free[0]),
                   emit_gu(1, P[GUB[1]], gu_free[1])]
        for ct in range(16):
            dnp = None
            if len(pend_h) == 2:
                dnp = flush_h(*pend_h.pop(0), defer_adds=True)
            bank = P[GUB[ct % 3]]
            te_gu = te_gu_q.pop(0)
            if ct + 2 < 16:
                te_gu_q.append(emit_gu(ct + 2, P[GUB[(ct + 2) % 3]],
                                       gu_free[(ct + 2) % 3]))
            sl_v = p.emit("scalar",
                          lambda e, b=bank, iw=IW: e.activation(
                              SILU[:, 0:iw], b[:, 0:iw], AF.Silu),
                          waits=[("te", te_gu)] +
                                ([("ve", hv_prev)] if hv_prev else []),
                          inc="sc")
            h_v = p.emit("vector",
                         lambda e, b=bank, iw=IW: e.tensor_tensor(
                             HT[:, 0:iw], SILU[:, 0:iw], b[:, iw:2 * iw],
                             OP.mult),
                         waits=[("sc", sl_v)], inc="ve")
            hv_prev = h_v
            gu_free[ct % 3] = [("sc", sl_v), ("ve", h_v)]
            vdrain()
            akey = "A0" if ct % 2 == 0 else "A1"
            qready, _ = quant_tile(HT[:, 0:IW], IW, [("ve", h_v)],
                                   q_waits=q_guard(akey),
                                   qo=QS_q[:, (ct % 2) * IW:(ct % 2 + 1) * IW])
            if dnp is not None:
                emit_adds(*dnp)
            pend_h.append((ct, akey, qready))
        i_off += IW
    for pe in pend_h:
        flush_h(*pe)
    pend_h = []

    ffn_deps = [("ve", p.t("ve"))]
    for ct in range(16):
        ln_tile(Xv[:, ct, :], ffn_deps)
        ffn_deps = []
        p.emit("sync",
               lambda e, c=ct: e.dma_start(out=dout_v[:, c, :], in_=Xv[:, c, :]),
               waits=[("ve", p.t("ve")), ("sc", p.t("sc"))],
               inc="st", inc_n=16)
    st_total = p.t("st")
    p.emit("sync", lambda e: e.wait_ge(p.sems["st"], st_total))

    build_program.counts = dict(p.cnt)
    build_program.nops = {e: len(v) for e, v in p.ops.items()}
    with p.es, nc.Block() as block:
        @block.sync
        def _(sync):
            for op in p.ops["sync"]:
                op(sync)

        @block.gpsimd
        def _(gpsimd):
            for op in p.ops["gpsimd"]:
                op(gpsimd)

        @block.vector
        def _(vector):
            for op in p.ops["vector"]:
                op(vector)

        @block.scalar
        def _(scalar):
            for op in p.ops["scalar"]:
                op(scalar)

        @block.tensor
        def _(tensor):
            for op in p.ops["tensor"]:
                op(tensor)

    return nc


_NC_CACHE = None


def kernel(L0_attn_x, L0_attn_wq, L0_attn_wk, L0_attn_wv, L0_attn_wo,
           L0_ffn_w_gate, L0_ffn_w_up, L0_ffn_w_down):
    global _NC_CACHE
    x = np.ascontiguousarray(np.asarray(L0_attn_x, dtype=np.float32))
    wq = np.ascontiguousarray(np.asarray(L0_attn_wq, dtype=np.float32))
    wk = np.ascontiguousarray(np.asarray(L0_attn_wk, dtype=np.float32))
    wv = np.ascontiguousarray(np.asarray(L0_attn_wv, dtype=np.float32))
    wo = np.ascontiguousarray(np.asarray(L0_attn_wo, dtype=np.float32))
    wg = np.ascontiguousarray(np.asarray(L0_ffn_w_gate, dtype=np.float32))
    wu = np.ascontiguousarray(np.asarray(L0_ffn_w_up, dtype=np.float32))
    wd = np.ascontiguousarray(np.asarray(L0_ffn_w_down, dtype=np.float32))

    if _NC_CACHE is None:
        _NC_CACHE = build_program()
    nc = _NC_CACHE

    in_maps = []
    for c in range(8):
        b, h = c // 2, c % 2
        own = x[b, h * T:(h + 1) * T, :]
        pre = x[b, 0:T, :]
        fl = np.full((128, 1), float(h), dtype=np.float32)
        in_maps.append({
            "x_own": np.ascontiguousarray(own),
            "x_pre": np.ascontiguousarray(pre),
            "flag": fl,
            "wq": wq, "wk": wk, "wv": wv, "wo": wo,
            "w_gate": wg, "w_up": wu, "w_down": wd,
        })
    res = run_bass_kernel_spmd(nc, in_maps, core_ids=list(range(8)))
    out = np.empty((Bq, Sq, D), dtype=np.float32)
    for c in range(8):
        b, h = c // 2, c % 2
        out[b, h * T:(h + 1) * T, :] = np.asarray(res.results[c]["out"],
                                                  dtype=np.float32)
    return out


# version 22
# speedup vs baseline: 1.0447x; 1.0334x over previous
import contextlib
import numpy as np
import concourse.bass as bass
import concourse.mybir as mybir
from concourse.bass_utils import run_bass_kernel_spmd

F32 = mybir.dt.float32
BF16 = mybir.dt.bfloat16
AF = mybir.ActivationFunctionType
OP = mybir.AluOpType
AX = mybir.AxisListType.X

Bq, Sq, D, H, HD = 4, 4096, 1024, 16, 64
T = 2048
SC = 512
NSC = T // SC
CH = 128
DI = 2816
ITILE = 256
EPS_A = 1e-6
EPS_LN = 1e-5
C12 = float(np.float32(1e-12))
C16 = float(1.0 / 6.0)

ENGINES = ("sync", "gpsimd", "vector", "scalar", "tensor")
STAGE = 3
SUB = 4


class Prog:
    def __init__(self, nc):
        self.nc = nc
        self.ops = {e: [] for e in ENGINES}
        self.sems = {}
        self.cnt = {}
        self.es = contextlib.ExitStack()

    def sem(self, name):
        self.sems[name] = self.es.enter_context(self.nc.semaphore(name))
        self.cnt[name] = 0

    def sb(self, name, shape, dt=F32):
        return self.es.enter_context(self.nc.sbuf_tensor(name, shape, dt))

    def ps(self, name):
        return self.es.enter_context(self.nc.psum_tensor(name, [128, 512], F32))

    def emit(self, eng, fn, waits=(), inc=None, inc_n=1):
        waits = tuple((s, v) for s, v in waits if v > 0)
        sems = self.sems

        def op(e, fn=fn, waits=waits, inc=inc, inc_n=inc_n):
            for s, v in waits:
                e.wait_ge(sems[s], v)
            r = fn(e)
            if inc is not None:
                r.then_inc(sems[inc], inc_n)

        self.ops[eng].append(op)
        if inc is not None:
            self.cnt[inc] += inc_n
            return self.cnt[inc]
        return None

    def t(self, sem):
        return self.cnt[sem]


def build_program():
    nc = bass.Bass("TRN2", target_bir_lowering=False)
    p = Prog(nc)

    dx = nc.dram_tensor("x_own", [T, D], F32, kind="ExternalInput").ap()
    dxp = nc.dram_tensor("x_pre", [T, D], F32, kind="ExternalInput").ap()
    dflag = nc.dram_tensor("flag", [128, 1], F32, kind="ExternalInput").ap()
    dwq = nc.dram_tensor("wq", [D, D], F32, kind="ExternalInput").ap()
    dwk = nc.dram_tensor("wk", [D, D], F32, kind="ExternalInput").ap()
    dwv = nc.dram_tensor("wv", [D, D], F32, kind="ExternalInput").ap()
    dwo = nc.dram_tensor("wo", [D, D], F32, kind="ExternalInput").ap()
    dwg = nc.dram_tensor("w_gate", [DI, D], F32, kind="ExternalInput").ap()
    dwu = nc.dram_tensor("w_up", [DI, D], F32, kind="ExternalInput").ap()
    dwd = nc.dram_tensor("w_down", [D, DI], F32, kind="ExternalInput").ap()
    dout = nc.dram_tensor("out", [T, D], F32, kind="ExternalOutput").ap()
    DBG = globals().get("DBGMODE", 0)
    if DBG:
        ddq = nc.dram_tensor("dbg_q", [128, 4096], F32, kind="ExternalOutput").ap()
        ddk = nc.dram_tensor("dbg_k", [128, 4096], F32, kind="ExternalOutput").ap()
        ddv = nc.dram_tensor("dbg_v", [128, 4096], F32, kind="ExternalOutput").ap()
        dda = nc.dram_tensor("dbg_ao", [128, 4096], F32, kind="ExternalOutput").ap()
        dds = nc.dram_tensor("dbg_s", [128, 520], F32, kind="ExternalOutput").ap()

    dx_v = dx.rearrange("(c p) f -> p c f", p=128)
    dxp_v = dxp.rearrange("(c p) f -> p c f", p=128)
    dout_v = dout.rearrange("(c p) f -> p c f", p=128)

    X = p.sb("X", [128, 16 * D])
    Xv = X[:].rearrange("p (c f) -> p c f", f=D)
    flag = p.sb("flagb", [128, 1])
    mask = p.sb("mask", [128, 128])
    ident = p.sb("ident", [128, 128])
    omeps = p.sb("omeps", [128, 1])
    LNm = p.sb("LNm", [128, 1])
    LNs = p.sb("LNs", [128, 1])
    LNv = p.sb("LNv", [128, 1])
    SCR = p.sb("SCR", [128, 1024])
    SCR2 = p.sb("SCR2", [128, 1024])
    DEN = p.sb("DEN", [128, 16])
    RDEN = p.sb("RDEN", [128, 16])
    FL = p.sb("FL", [128, 16])

    ARENA = p.sb("ARENA", [128, 34240])
    AB = ARENA[:]

    def cv32(off_b, n):
        assert off_b % 32 == 0 and off_b + 4 * n <= 136960
        return AB[:, off_b // 4: off_b // 4 + n]

    def cv16(off_b, n):
        assert off_b % 32 == 0 and n % 2 == 0 and off_b + 2 * n <= 136960
        return AB[:, off_b // 4: off_b // 4 + n // 2].bitcast(BF16)

    NW = 2
    wraw = [cv32(0, 2048), cv32(8192, 2048)]
    whi = [cv16(16384, 2048), cv16(20480, 2048)]
    wlo = [cv16(24576, 2048), cv16(28672, 2048)]
    Xf_hi = cv16(32768, 4096)
    Xf_lo = cv16(40960, 4096)
    Qf = cv32(49152, 4096)
    Kf = cv32(65536, 4096)
    Vf = cv32(81920, 4096)
    Ktm = cv32(98304, 4096)
    Vex = cv32(114688, 4160)
    AT = cv32(131328, 128)
    S_ext = cv32(131840, 520)
    Sv = S_ext.rearrange("p (j e) -> p j e", e=65)
    AO = Vf
    PW_hi = Xf_hi
    PW_lo = Xf_lo
    NUM = SCR2[:, 0:1024]

    XQ_hi = cv16(0, 16384)
    XQ_lo = cv16(32768, 16384)
    QS_t = cv32(65536, 64)
    QS_s = cv32(65792, 64)
    QS_r = cv32(66048, 64)
    QS_xn = cv32(66304, 1024)
    QS_a = cv32(70400, 1024)
    QS_b = cv32(74496, 1024)
    QS_sg = cv32(78592, 1024)
    QS_q = cv32(82688, 1024)
    QS_q2 = cv32(86784, 1024)
    WGU_hi = cv16(90880, 8 * 2 * ITILE)
    WGU_lo = cv16(99072, 8 * 2 * ITILE)
    WD_hi = cv16(107264, 2 * D)
    WD_lo = cv16(111360, 2 * D)
    FW = cv32(115456, 2 * D)
    HT = cv32(131840, ITILE)
    HF_hi = cv16(133888, 2 * CH)
    HF_lo = cv16(134400, 2 * CH)
    SILU = cv32(134912, ITILE)

    P = [p.ps(f"P{i}") for i in range(8)]

    for s in ("ld", "st", "ve", "sc", "gp", "te"):
        p.sem(s)

    def vdrain(hard=False):
        if hard:
            p.emit("vector", lambda e: e.drain())
        else:
            p.emit("vector", lambda e: e.tensor_copy(FL[:, 8:16], FL[:, 0:8]))

    pb_freed = {i: [] for i in range(8)}

    ld_x = p.emit("sync", lambda e: e.dma_start(out=Xv, in_=dx_v), inc="ld", inc_n=16)
    ld_fl = p.emit("sync", lambda e: e.dma_start(out=flag[:], in_=dflag),
                   inc="ld", inc_n=16)

    def init_const(e):
        e.memset(mask[:], 1.0)
        e.affine_select(mask[:], mask[:], [[1, 128]], OP.is_ge, 0.0,
                        base=0, channel_multiplier=-1)
        e.memset(ident[:], 1.0)
        e.affine_select(ident[:], ident[:], [[-1, 128]], OP.is_equal, 0.0,
                        base=0, channel_multiplier=1)
        e.memset(S_ext[:], 0.0)
        return e.memset(
            Vex[:].rearrange("p (c h e) -> p c h e", h=16, e=65)[:, :, :, 64:65], 1.0)
    gp_const = p.emit("gpsimd", init_const, inc="gp")

    def init_omeps(e):
        e.memset(FL[:], 0.0)
        e.memset(omeps[:], 1.0)
        return e.tensor_single_scalar(omeps[:], omeps[:], float(2.0 ** -46),
                                      OP.subtract)
    p.emit("vector", init_omeps, inc="ve")
    vdrain()

    def transpose_x_to_fm(xsrc_v, base_tile, deps):
        xh = Xf_hi[:].rearrange("p (g t) -> p g t", t=SC)
        xl = Xf_lo[:].rearrange("p (g t) -> p g t", t=SC)
        first = list(deps)
        for tc in range(4):
            ctile = base_tile + tc
            for fc in range(8):
                bk = 6 + (tc * 8 + fc) % 2
                bank = P[bk]
                te_v = p.emit(
                    "tensor",
                    lambda e, ct=ctile, f=fc, b=bank: e.transpose(
                        b[:, 0:128], xsrc_v[:, ct, f * 128:(f + 1) * 128], ident[:]),
                    waits=first + pb_freed[bk] + [("gp", gp_const)],
                    inc="te")
                first = []
                hi_v = p.emit(
                    "vector",
                    lambda e, f=fc, tc=tc, b=bank: e.tensor_copy(
                        xh[:, f, tc * 128:(tc + 1) * 128], b[:, 0:128]),
                    waits=[("te", te_v)], inc="ve")
                lo_v = p.emit(
                    "vector",
                    lambda e, f=fc, tc=tc, b=bank: e.tensor_tensor(
                        xl[:, f, tc * 128:(tc + 1) * 128], b[:, 0:128],
                        xh[:, f, tc * 128:(tc + 1) * 128], OP.subtract),
                    inc="ve")
                pb_freed[bk] = [("ve", lo_v)]
        return [("ve", p.t("ve"))]

    def stream_w_echunk(dw, off, buf_i, deps):
        src = dw[:, off:off + 256].rearrange("(g p) e -> p g e", p=128)
        dst = wraw[buf_i][:].rearrange("p (g e) -> p g e", e=256)
        ldv = p.emit("sync", lambda e, s=src, d=dst: e.dma_start(out=d, in_=s),
                     waits=list(deps), inc="ld", inc_n=16)
        hi_v = p.emit("scalar",
                      lambda e, i=buf_i: e.copy(whi[i][:], wraw[i][:]),
                      waits=[("ld", ldv), ("te", p.t("te"))], inc="sc")
        lo_v = p.emit("gpsimd",
                      lambda e, i=buf_i: e.tensor_sub(wlo[i][:], wraw[i][:], whi[i][:]),
                      waits=[("sc", hi_v)], inc="gp")
        return [("gp", lo_v), ("sc", hi_v)]

    pj_state = {"bi": 0}

    def project_one(dstbuf, dw, xf_deps):
        xh = Xf_hi[:].rearrange("p (g t) -> p g t", t=SC)
        xl = Xf_lo[:].rearrange("p (g t) -> p g t", t=SC)
        wdeps = {0: [("te", p.t("te"))], 1: [("te", p.t("te"))]}
        bi = pj_state["bi"]
        dstv = dstbuf[:].rearrange("p (g t) -> p g t", t=SC)
        for ec in range(4):
            wd = stream_w_echunk(dw, ec * 256, bi, wdeps[bi])
            for sub in range(2):
                ecg = ec * 2 + sub
                bank = P[ecg % 2]

                def mmfn(e, bi=bi, sub=sub, b=bank):
                    wh = whi[bi][:].rearrange("p (g e) -> p g e", e=256)
                    wl = wlo[bi][:].rearrange("p (g e) -> p g e", e=256)
                    r = None
                    for f in range(8):
                        sth = wh[:, f, sub * 128:(sub + 1) * 128]
                        stl = wl[:, f, sub * 128:(sub + 1) * 128]
                        e.matmul(b[:], sth, xh[:, f, :], start=(f == 0),
                                 stop=False)
                        e.matmul(b[:], sth, xl[:, f, :], start=False, stop=False)
                        r = e.matmul(b[:], stl, xh[:, f, :], start=False,
                                     stop=(f == 7))
                    return r
                te_v = p.emit("tensor", mmfn,
                              waits=wd + xf_deps + pb_freed[ecg % 2],
                              inc="te")
                xf_deps = []
                dst = dstv[:, ecg, :]
                if ecg % 2 == 0:
                    ev = p.emit("vector",
                                lambda e, d=dst, b=bank: e.tensor_copy(
                                    d, b[:]),
                                waits=[("te", te_v)], inc="ve")
                    pb_freed[0] = [("ve", ev)]
                else:
                    ev = p.emit("scalar",
                                lambda e, d=dst, b=bank: e.copy(d, b[:]),
                                waits=[("te", te_v)], inc="sc")
                    pb_freed[1] = [("sc", ev)]
            wdeps[bi] = [("te", p.t("te"))]
            bi ^= 1
        pj_state["bi"] = bi
        return [("ve", p.t("ve")), ("sc", p.t("sc"))]

    def fm_one(buf, deps):
        ex_prev = 0
        first = list(deps)
        for q4 in range(4):
            sl0, sl1 = q4 * 1024, (q4 + 1) * 1024
            mn = p.emit("vector",
                        lambda e, b=buf, s0=sl0, s1=sl1: e.tensor_scalar_min(
                            SCR[:], b[:, s0:s1], 0.0),
                        waits=first + ([("sc", ex_prev)] if ex_prev else []),
                        inc="ve")
            first = []
            vdrain()
            ex_prev = p.emit("scalar",
                             lambda e: e.activation(SCR2[:], SCR[:], AF.Exp),
                             waits=[("ve", mn)], inc="sc")
            p.emit("vector",
                   lambda e, b=buf, s0=sl0, s1=sl1: e.scalar_tensor_tensor(
                       b[:, s0:s1], b[:, s0:s1], 0.0, SCR2[:], OP.max, OP.add),
                   waits=[("sc", ex_prev)], inc="ve")
        return [("ve", p.t("ve"))]

    def transpose_kv(v_deps, k_deps):
        vex = Vex[:].rearrange("p (c h e) -> p c h e", h=16, e=65)
        ktm = Ktm[:].rearrange("p (c f) -> p c f", f=D)
        kf = Kf[:].rearrange("p (g t) -> p g t", t=SC)
        vf = Vf[:].rearrange("p (g t) -> p g t", t=SC)
        n = 0
        first = list(v_deps)
        for tc in range(4):
            for fc in range(8):
                bk = 6 + n % 2
                bank = P[bk]
                n += 1
                srcf = vf[:, fc, tc * 128:(tc + 1) * 128]
                te_v = p.emit("tensor",
                              lambda e, s=srcf, b=bank: e.transpose(
                                  b[:, 0:128], s, ident[:]),
                              waits=first + pb_freed[bk] + [("gp", gp_const)],
                              inc="te")
                first = []
                ev = p.emit("vector",
                            lambda e, f=fc, t=tc, b=bank: e.tensor_copy(
                                vex[:, t, 2 * f:2 * f + 2, 0:64],
                                b[:, 0:128].rearrange(
                                    "p (h d) -> p h d", d=64)),
                            waits=[("te", te_v)], inc="ve")
                pb_freed[bk] = [("ve", ev)]
        first = list(k_deps)
        for tc in range(4):
            for fc in range(8):
                bk = 6 + n % 2
                bank = P[bk]
                n += 1
                srcf = kf[:, fc, tc * 128:(tc + 1) * 128]
                te_v = p.emit("tensor",
                              lambda e, s=srcf, b=bank: e.transpose(
                                  b[:, 0:128], s, ident[:]),
                              waits=first + pb_freed[bk] + [("gp", gp_const)],
                              inc="te")
                first = []
                ev = p.emit("vector",
                            lambda e, f=fc, t=tc, b=bank: e.tensor_copy(
                                ktm[:, t, f * 128:(f + 1) * 128],
                                b[:, 0:128]),
                            waits=[("te", te_v)], inc="ve")
                pb_freed[bk] = [("ve", ev)]
        return [("ve", p.t("ve"))]

    def state_update(tc_range, kv_deps):
        ktm = Ktm[:].rearrange("p (c f) -> p c f", f=D)
        vex = Vex[:].rearrange("p (c h e) -> p c h e", h=16, e=65)
        first = list(kv_deps)
        for tc in tc_range:
            for h in range(16):
                fg, off = h // 2, (h % 2) * 64
                bk = 2 + h % 2
                bank = P[bk]
                te_v = p.emit(
                    "tensor",
                    lambda e, tc=tc, fg=fg, off=off, h=h, b=bank: e.matmul(
                        b[off:off + 64, 0:65],
                        ktm[:, tc, fg * 128 + off:fg * 128 + off + 64],
                        vex[:, tc, h, :], start=True, stop=True),
                    waits=first + pb_freed[bk], inc="te")
                first = []
                add = p.emit("vector",
                             lambda e, fg=fg, off=off, b=bank: e.tensor_add(
                                 Sv[off:off + 64, fg, :],
                                 Sv[off:off + 64, fg, :],
                                 b[off:off + 64, 0:65]),
                             waits=[("te", te_v)], inc="ve")
                pb_freed[bk] = [("ve", add)]
        return [("ve", p.t("ve"))]

    def attention_chunks(kv_deps, q_deps):
        vex = Vex[:].rearrange("p (c h e) -> p c h e", h=16, e=65)
        qf = Qf[:].rearrange("p (g t) -> p g t", t=SC)
        kf = Kf[:].rearrange("p (g t) -> p g t", t=SC)
        ao = AO[:].rearrange("p (c f) -> p c f", f=D)
        numv = NUM[:].rearrange("p (h d) -> p h d", d=64)
        ATb = [AT[:], SCR[:, 0:128]]
        first = list(kv_deps) + list(q_deps)

        def emit_at(tc, h, extra):
            fg, off = h // 2, (h % 2) * 64
            bk = h % 2

            def at_fn(e, fg=fg, off=off, tc=tc, b=P[bk]):
                return e.matmul(
                    b[:, 0:128],
                    kf[:, fg, tc * 128:(tc + 1) * 128][off:off + 64, :],
                    qf[:, fg, tc * 128:(tc + 1) * 128][off:off + 64, :],
                    start=True, stop=True)
            return p.emit("tensor", at_fn,
                          waits=extra + pb_freed[bk] + [("gp", gp_const)],
                          inc="te")

        for tc in range(4):
            te_at_next = emit_at(tc, 0, first)
            first = []
            for h in range(16):
                fg, off = h // 2, (h % 2) * 64
                abk = h % 2
                nd_bank = P[4 if h % 2 == 0 else 5]
                nd_off = ((h // 2) % 4) * 128
                te_at = te_at_next
                if h + 1 < 16:
                    te_at_next = emit_at(tc, h + 1, [])
                ve_m = p.emit("vector",
                              lambda e, b=P[abk], d=ATb[h % 2]:
                              e.scalar_tensor_tensor(
                                  d, b[:, 0:128], 1.0, mask[:],
                                  OP.mult, OP.mult),
                              waits=[("te", te_at), ("gp", gp_const)],
                              inc="ve")
                pb_freed[abk] = [("ve", ve_m)]

                def nd_fn(e, h=h, fg=fg, off=off, tc=tc, b=nd_bank, no=nd_off,
                          at_s=ATb[h % 2]):
                    sl = b[:, no:no + 65]
                    e.matmul(sl, at_s, vex[:, tc, h, :], start=True, stop=False)
                    return e.matmul(
                        sl, qf[:, fg, tc * 128:(tc + 1) * 128][off:off + 64, :],
                        Sv[off:off + 64, fg, :], start=False, stop=True)
                te_nd = p.emit("tensor", nd_fn, waits=[("ve", ve_m)], inc="te")
                nv = p.emit("vector",
                            lambda e, h=h, b=nd_bank, no=nd_off: e.tensor_copy(
                                numv[:, h, :], b[:, no:no + 64]),
                            waits=[("te", te_nd)], inc="ve")
                dv = p.emit("vector",
                            lambda e, h=h, b=nd_bank, no=nd_off: e.tensor_copy(
                                DEN[:, h:h + 1], b[:, no + 64:no + 65]),
                            inc="ve")
                pb_freed[4 + abk] = [("ve", dv)]
            vdrain()
            p.emit("vector",
                   lambda e: e.tensor_single_scalar(DEN[:], DEN[:], EPS_A, OP.add),
                   inc="ve")
            vdrain()
            p.emit("vector", lambda e: e.reciprocal(RDEN[:], DEN[:]), inc="ve")
            vdrain()
            p.emit("vector",
                   lambda e, tc=tc: e.tensor_tensor(
                       ao[:, tc, :].rearrange("p (h d) -> p h d", d=64),
                       numv,
                       RDEN[:].unsqueeze(2).broadcast_to([128, 16, 64]), OP.mult),
                   inc="ve")
            vdrain()
            state_update([tc], [])
        return [("ve", p.t("ve"))]

    def ln_tile(xap, deps):
        p.emit("vector", lambda e: e.reduce_sum(LNm[:], xap, AX),
               waits=list(deps), inc="ve")
        vdrain()
        sq = p.emit("scalar",
                    lambda e: e.activation(SCR2[:], xap, AF.Square,
                                           accum_out=LNs[:]),
                    waits=list(deps) + [("ve", p.t("ve"))], inc="sc")
        p.emit("vector",
               lambda e: e.tensor_single_scalar(LNm[:], LNm[:], 1.0 / D, OP.mult),
               inc="ve")
        vdrain()
        p.emit("vector", lambda e: e.tensor_tensor(LNv[:], LNm[:], LNm[:], OP.mult),
               waits=[("sc", sq)], inc="ve")
        p.emit("vector",
               lambda e: e.tensor_single_scalar(SCR[:, 0:1], LNs[:], 1.0 / D,
                                                OP.mult),
               inc="ve")
        vdrain()
        p.emit("vector",
               lambda e: e.tensor_tensor(LNv[:], SCR[:, 0:1], LNv[:], OP.subtract),
               inc="ve")
        vdrain()
        p.emit("vector",
               lambda e: e.tensor_single_scalar(LNv[:], LNv[:], EPS_LN, OP.add),
               inc="ve")
        vdrain()
        rt = p.emit("scalar", lambda e: e.activation(LNs[:], LNv[:], AF.Sqrt),
                    waits=[("ve", p.t("ve"))], inc="sc")
        p.emit("vector", lambda e: e.reciprocal(LNv[:], LNs[:]),
               waits=[("sc", rt)], inc="ve")
        vdrain()
        v = p.emit("vector",
                   lambda e: e.tensor_scalar(xap, xap, LNm[0:128, 0:1],
                                             LNv[0:128, 0:1], OP.subtract, OP.mult),
                   inc="ve")
        vdrain()
        return v

    def wo_and_ln1(sc_idx, ao_deps):
        ao = AO[:].rearrange("p (c f) -> p c f", f=D)
        pwh = PW_hi[:].rearrange("p (g t) -> p g t", t=SC)
        pwl = PW_lo[:].rearrange("p (g t) -> p g t", t=SC)
        first = list(ao_deps)
        pw_deps = []
        for tc in range(4):
            dsth = [pwh[:, fc, tc * 128:(tc + 1) * 128] for fc in range(8)]
            dstl = [pwl[:, fc, tc * 128:(tc + 1) * 128] for fc in range(8)]
            pw_deps, _ = transpose_split(ao[:, tc, :], 8, dsth, dstl, first)
            first = []
        if SUB < 3:
            return pw_deps
        for half in range(2):
            wd0 = stream_w_echunk(dwo, half * 512, 0, pw_deps)
            wd1 = stream_w_echunk(dwo, half * 512 + 256, 1, pw_deps)
            for tc in range(4):
                ctile = sc_idx * 4 + tc
                bank = P[2 if tc % 2 == 0 else 3]

                def mmfn(e, tc=tc, b=bank):
                    whs = [whi[0][:].rearrange("p (g e) -> p g e", e=256),
                           whi[1][:].rearrange("p (g e) -> p g e", e=256)]
                    wls = [wlo[0][:].rearrange("p (g e) -> p g e", e=256),
                           wlo[1][:].rearrange("p (g e) -> p g e", e=256)]
                    r = None
                    for piece in range(2):
                        bsl = b[:, piece * 256:piece * 256 + 256]
                        for g in range(8):
                            sth = pwh[:, g, tc * 128:(tc + 1) * 128]
                            stl = pwl[:, g, tc * 128:(tc + 1) * 128]
                            e.matmul(bsl, sth, whs[piece][:, g, :],
                                     start=(g == 0), stop=False)
                            e.matmul(bsl, sth, wls[piece][:, g, :],
                                     start=False, stop=False)
                            r = e.matmul(bsl, stl, whs[piece][:, g, :],
                                         start=False, stop=(g == 7))
                    return r
                te_v = p.emit("tensor", mmfn,
                              waits=wd0 + wd1 + pw_deps + pb_freed[2 + tc % 2],
                              inc="te")
                add = p.emit("vector",
                             lambda e, ct=ctile, hf=half, b=bank: e.tensor_add(
                                 Xv[:, ct, hf * 512:(hf + 1) * 512],
                                 Xv[:, ct, hf * 512:(hf + 1) * 512], b[:]),
                             waits=[("te", te_v)], inc="ve")
                pb_freed[2 + tc % 2] = [("ve", add)]
            pw_deps = [("te", p.t("te"))]
        if SUB < 4:
            return [("ve", p.t("ve"))]
        for tc in range(4):
            ln_tile(Xv[:, sc_idx * 4 + tc, :], [])
        return [("ve", p.t("ve"))]

    def transpose_split(src, n, dsth, dstl, deps):
        first = list(deps)
        for i in range(n):
            bk = 6 + (i % 2)
            bank = P[bk]
            te_v = p.emit("tensor",
                          lambda e, i=i, b=bank: e.transpose(
                              b[:, 0:128], src[:, i * 128:(i + 1) * 128],
                              ident[:]),
                          waits=first + pb_freed[bk] + [("gp", gp_const)],
                          inc="te")
            first = []
            hi = p.emit("scalar",
                        lambda e, i=i, b=bank: e.copy(dsth[i], b[:, 0:128]),
                        waits=[("te", te_v)], inc="sc")
            lo = p.emit("vector",
                        lambda e, i=i, b=bank: e.tensor_tensor(
                            dstl[i], b[:, 0:128], dsth[i], OP.subtract),
                        waits=[("sc", hi)], inc="ve")
            pb_freed[bk] = [("sc", hi), ("ve", lo)]
        return [("ve", p.t("ve")), ("sc", p.t("sc"))], [("te", p.t("te"))]

    def quant_tile(xap, F, deps, q_waits=(), qo=None):
        Bn = F // 16
        if qo is None:
            qo = QS_q[:, 0:F]
        x3 = xap.rearrange("p (b k) -> p b k", k=16)
        p.emit("vector",
               lambda e: e.tensor_reduce(QS_t[:, 0:Bn], x3, AX, OP.max,
                                         apply_absolute_value=True),
               waits=list(deps), inc="ve")
        vdrain()
        p.emit("vector",
               lambda e: e.tensor_scalar(QS_s[:, 0:Bn], QS_t[:, 0:Bn], C16, C12,
                                         OP.mult, OP.max), inc="ve")
        vdrain()
        p.emit("vector", lambda e: e.reciprocal(QS_r[:, 0:Bn], QS_s[:, 0:Bn]),
               inc="ve")
        vdrain()
        vdrain()
        p.emit("vector",
               lambda e: e.tensor_tensor(QS_r[:, 0:Bn], QS_r[:, 0:Bn],
                                         omeps[:].broadcast_to([128, Bn]),
                                         OP.mult), inc="ve")
        vdrain()
        ua = p.emit("scalar", lambda e: e.activation(QS_a[:, 0:F], xap, AF.Abs),
                    waits=list(deps) + [("ve", p.t("ve"))], inc="sc")
        sg = p.emit("scalar", lambda e: e.activation(QS_sg[:, 0:F], xap, AF.Sign),
                    inc="sc")
        p.emit("vector",
               lambda e: e.tensor_tensor(
                   QS_xn[:, 0:F].rearrange("p (b k) -> p b k", k=16),
                   QS_a[:, 0:F].rearrange("p (b k) -> p b k", k=16),
                   QS_r[:, 0:Bn].unsqueeze(2).broadcast_to([128, Bn, 16]),
                   OP.mult),
               waits=[("sc", ua)], inc="ve")
        vdrain()
        p.emit("vector",
               lambda e: e.tensor_single_scalar(QS_a[:, 0:F], QS_xn[:, 0:F],
                                                0.25, OP.is_gt), inc="ve")
        vdrain()
        for m in (0.75, 1.25, 1.75):
            p.emit("vector",
                   lambda e, m=m: e.scalar_tensor_tensor(
                       QS_a[:, 0:F], QS_xn[:, 0:F], m, QS_a[:, 0:F],
                       OP.is_gt, OP.add), inc="ve")
        p.emit("vector",
               lambda e: e.tensor_scalar(QS_b[:, 0:F], QS_xn[:, 0:F], 5.0, 2.0,
                                         OP.is_gt, OP.mult), inc="ve")
        vdrain()
        for m in (2.5, 3.5):
            p.emit("vector",
                   lambda e, m=m: e.scalar_tensor_tensor(
                       QS_b[:, 0:F], QS_xn[:, 0:F], m, QS_b[:, 0:F],
                       OP.is_gt, OP.add), inc="ve")
        cb = p.emit("vector",
                    lambda e: e.scalar_tensor_tensor(
                        QS_a[:, 0:F], QS_a[:, 0:F], 0.5, QS_b[:, 0:F],
                        OP.mult, OP.add), inc="ve")
        p.emit("vector",
               lambda e: e.tensor_mul(QS_b[:, 0:F], QS_a[:, 0:F],
                                      QS_sg[:, 0:F]),
               waits=[("sc", sg)], inc="ve")
        xq = p.emit("vector",
                    lambda e: e.tensor_tensor(
                        qo.rearrange("p (b k) -> p b k", k=16),
                        QS_b[:, 0:F].rearrange("p (b k) -> p b k", k=16),
                        QS_s[:, 0:Bn].unsqueeze(2).broadcast_to([128, Bn, 16]),
                        OP.mult),
                    waits=list(q_waits), inc="ve")
        return [("ve", xq)], [("sc", sg), ("ve", xq)]

    ao_v = AO[:].rearrange("p (c f) -> p c f", f=D)
    for psc in range(NSC if STAGE >= 1 else 0):
        ldp = p.emit("sync",
                     lambda e, s=psc: e.dma_start(
                         out=ao_v, in_=dxp_v[:, s * 4:(s + 1) * 4, :]),
                     waits=[("te", p.t("te")), ("ve", p.t("ve"))],
                     inc="ld", inc_n=16)
        xf = transpose_x_to_fm(ao_v, 0, [("ld", ldp)])
        kd = project_one(Kf, dwk, xf)
        fmk = fm_one(Kf, kd)
        vd = project_one(Vf, dwv, [])
        kv = transpose_kv(vd, fmk)
        state_update(range(4), kv)

    p.emit("vector",
           lambda e: e.tensor_tensor(S_ext[:], S_ext[:],
                                     flag[:].broadcast_to([128, 520]), OP.mult),
           waits=[("ld", ld_fl), ("ve", p.t("ve"))], inc="ve")
    vdrain()

    def dbg_dump(dst, srcap):
        p.emit("sync", lambda e, d=dst, s=srcap: e.dma_start(out=d, in_=s),
               waits=[("ve", p.t("ve")), ("sc", p.t("sc")), ("gp", p.t("gp")),
                      ("te", p.t("te"))],
               inc="st", inc_n=16)
        stv = p.t("st")
        p.emit("vector", lambda e: e.nop(), waits=[("st", stv)], inc="ve")
        p.emit("scalar", lambda e: e.nop(), waits=[("st", stv)], inc="sc")

    for sc in range(NSC if STAGE >= 2 else 0):
        xf = transpose_x_to_fm(Xv, sc * 4, [("ld", ld_x)])
        qd = project_one(Qf, dwq, xf)
        fmq = fm_one(Qf, qd)
        kd = project_one(Kf, dwk, [])
        fmk = fm_one(Kf, kd)
        vd = project_one(Vf, dwv, [])
        fm = [("ve", p.t("ve"))]
        kv = transpose_kv(vd, fmk)
        if DBG and sc == 0:
            dbg_dump(ddq, Qf)
            dbg_dump(ddk, Kf)
            dbg_dump(ddv, Vf)
            dbg_dump(dds, S_ext)
        if SUB >= 1:
            at = attention_chunks(kv, fm)
        else:
            at = kv
        if DBG and sc == 0 and SUB >= 1:
            dbg_dump(dda, AO)
        if SUB >= 2:
            wo_and_ln1(sc, at)

    ph1 = [(s, p.t(s)) for s in ("ld", "ve", "sc", "gp", "te")]
    xq_h = XQ_hi[:].rearrange("p (g t) -> p g t", t=T)
    xq_l = XQ_lo[:].rearrange("p (g t) -> p g t", t=T)
    qdeps = list(ph1)
    qfree = {"A0": 0, "A1": 0, "B": 0}
    QBUF = {"A": QS_q, "B": QS_q2}

    def q_guard(buf):
        if buf == "A":
            return [("te", max(qfree["A0"], qfree["A1"]))]
        return [("te", qfree[buf])]

    def q_mark(buf, tev):
        if buf == "B":
            qfree["B"] = tev
        elif buf == "A":
            qfree["A0"] = qfree["A1"] = tev
        else:
            qfree[buf] = tev

    pend_x = None
    for ct in range(16 if STAGE >= 3 else 0):
        buf = "A" if ct % 2 == 0 else "B"
        qready, _ = quant_tile(Xv[:, ct, :], D, qdeps, q_waits=q_guard(buf),
                               qo=QBUF[buf][:, 0:D])
        qdeps = []
        if pend_x is not None:
            pct, pbuf, pqr = pend_x
            dsth = [xq_h[:, f, pct * 128:(pct + 1) * 128] for f in range(8)]
            dstl = [xq_l[:, f, pct * 128:(pct + 1) * 128] for f in range(8)]
            transpose_split(QBUF[pbuf][:, 0:D], 8, dsth, dstl, pqr)
            q_mark(pbuf, p.t("te"))
        pend_x = (ct, buf, qready)
    if pend_x is not None:
        pct, pbuf, pqr = pend_x
        dsth = [xq_h[:, f, pct * 128:(pct + 1) * 128] for f in range(8)]
        dstl = [xq_l[:, f, pct * 128:(pct + 1) * 128] for f in range(8)]
        transpose_split(QBUF[pbuf][:, 0:D], 8, dsth, dstl, pqr)
        q_mark(pbuf, p.t("te"))
    xq_deps = [("ve", p.t("ve")), ("sc", p.t("sc"))]

    itiles = [256] * 11 if STAGE >= 3 else []
    i_off = 0
    fw = FW[:].rearrange("p (d f) -> p d f", f=D)
    wgu_h = WGU_hi[:].rearrange("p (g i) -> p g i", i=2 * ITILE)
    wgu_l = WGU_lo[:].rearrange("p (g i) -> p g i", i=2 * ITILE)
    wd_h = WD_hi[:].rearrange("p (c o) -> p c o", o=D)
    wd_l = WD_lo[:].rearrange("p (c o) -> p c o", o=D)
    hf_h = HF_hi[:].rearrange("p (c t) -> p c t", t=CH)
    hf_l = HF_lo[:].rearrange("p (c t) -> p c t", t=CH)
    fwbuf_deps = {0: list(ph1), 1: list(ph1)}
    GUB = [0, 1, 4]
    gu_free = {0: [], 1: [], 2: []}
    dn_free = []
    hv_prev = 0
    pend_h = []
    NIC_C, IW_C = ITILE // 128, ITILE

    def flush_h(ct, akey, qready, defer_adds=False):
        nonlocal dn_free
        dsth = [hf_h[:, i, :] for i in range(NIC_C)]
        dstl = [hf_l[:, i, :] for i in range(NIC_C)]
        hf_deps, _ = transpose_split(
            QS_q[:, (ct % 2) * IW_C:(ct % 2 + 1) * IW_C], NIC_C, dsth, dstl,
            qready)
        q_mark(akey, p.t("te"))

        def dn_fn(e):
            r = None
            for half, b2 in ((0, P[2]), (1, P[3])):
                for i in range(NIC_C):
                    wdh = wd_h[:, i, half * 512:(half + 1) * 512]
                    wdl = wd_l[:, i, half * 512:(half + 1) * 512]
                    e.matmul(b2[:], hf_h[:, i, :], wdh, start=(i == 0),
                             stop=False)
                    e.matmul(b2[:], hf_h[:, i, :], wdl, start=False,
                             stop=False)
                    r = e.matmul(b2[:], hf_l[:, i, :], wdh, start=False,
                                 stop=(i == NIC_C - 1))
            return r
        te_dn = p.emit("tensor", dn_fn, waits=hf_deps + dn_free, inc="te")
        if defer_adds:
            return (ct, te_dn)
        emit_adds(ct, te_dn)
        return None

    def emit_adds(ctp, te_dn):
        nonlocal dn_free
        last_add = None
        for half, b2 in ((0, P[2]), (1, P[3])):
            last_add = p.emit("vector",
                              lambda e, c=ctp, hf=half, b=b2: e.tensor_add(
                                  Xv[:, c, hf * 512:(hf + 1) * 512],
                                  Xv[:, c, hf * 512:(hf + 1) * 512], b[:]),
                              waits=[("te", te_dn)], inc="ve")
        dn_free = [("ve", last_add)]

    for it_idx, IW in enumerate(itiles):
        NIC = IW // 128
        for pe in pend_h:
            flush_h(*pe)
        pend_h = []
        wtiles = ([("gu", wb, dw, ic) for wb, dw in ((0, dwg), (ITILE, dwu))
                   for ic in range(NIC)] + [("wd", oc) for oc in range(8)])
        pend_w = None
        for j, t in enumerate(wtiles):
            bi = j % 2
            buf = "A" if j % 2 == 0 else "B"
            Fw_ = D if t[0] == "gu" else IW
            if t[0] == "gu":
                _, wb, dw, ic = t
                row0 = i_off + ic * 128
                ld_v = p.emit("sync",
                              lambda e, r=row0, b=bi, d=dw: e.dma_start(
                                  out=fw[:, b, :], in_=d[r:r + 128, :]),
                              waits=fwbuf_deps[bi], inc="ld", inc_n=16)
            else:
                oc = t[1]
                ld_v = p.emit("sync",
                              lambda e, o=oc, b=bi, i0=i_off, iw=IW:
                              e.dma_start(out=fw[:, b, 0:iw],
                                          in_=dwd[o * 128:(o + 1) * 128,
                                                  i0:i0 + iw]),
                              waits=fwbuf_deps[bi], inc="ld", inc_n=16)
            qready, xdone = quant_tile(fw[:, bi, 0:Fw_], Fw_, [("ld", ld_v)],
                                       q_waits=q_guard(buf),
                                       qo=QBUF[buf][:, 0:Fw_])
            fwbuf_deps[bi] = xdone
            if pend_w is not None:
                pend_w()
            if t[0] == "gu":
                dsth = [wgu_h[:, f, wb + ic * 128:wb + (ic + 1) * 128]
                        for f in range(8)]
                dstl = [wgu_l[:, f, wb + ic * 128:wb + (ic + 1) * 128]
                        for f in range(8)]
                n_t = 8
            else:
                dsth = [wd_h[:, i, oc * 128:(oc + 1) * 128] for i in range(NIC)]
                dstl = [wd_l[:, i, oc * 128:(oc + 1) * 128] for i in range(NIC)]
                n_t = NIC

            def pend_w(buf=buf, Fw_=Fw_, n_t=n_t, dsth=dsth, dstl=dstl,
                       qready=qready):
                transpose_split(QBUF[buf][:, 0:Fw_], n_t, dsth, dstl, qready)
                q_mark(buf, p.t("te"))
        pend_w()
        wq_deps = [("ve", p.t("ve")), ("sc", p.t("sc"))]

        def emit_gu(ct, bank, extra):
            def gu_fn(e, ct=ct, b=bank):
                r = None
                for f in range(8):
                    sth = xq_h[:, f, ct * 128:(ct + 1) * 128]
                    stl = xq_l[:, f, ct * 128:(ct + 1) * 128]
                    e.matmul(b[:], sth, wgu_h[:, f, :], start=(f == 0),
                             stop=False)
                    e.matmul(b[:], sth, wgu_l[:, f, :], start=False, stop=False)
                    r = e.matmul(b[:], stl, wgu_h[:, f, :], start=False,
                                 stop=(f == 7))
                return r
            return p.emit("tensor", gu_fn, waits=wq_deps + xq_deps + extra,
                          inc="te")

        te_gu_q = [emit_gu(0, P[GUB[0]], gu_free[0]),
                   emit_gu(1, P[GUB[1]], gu_free[1])]
        for ct in range(16):
            dnp = None
            if len(pend_h) == 2:
                dnp = flush_h(*pend_h.pop(0), defer_adds=True)
            bank = P[GUB[ct % 3]]
            te_gu = te_gu_q.pop(0)
            if ct + 2 < 16:
                te_gu_q.append(emit_gu(ct + 2, P[GUB[(ct + 2) % 3]],
                                       gu_free[(ct + 2) % 3]))
            sl_v = p.emit("scalar",
                          lambda e, b=bank, iw=IW: e.activation(
                              SILU[:, 0:iw], b[:, 0:iw], AF.Silu),
                          waits=[("te", te_gu)] +
                                ([("ve", hv_prev)] if hv_prev else []),
                          inc="sc")
            h_v = p.emit("vector",
                         lambda e, b=bank, iw=IW: e.tensor_tensor(
                             HT[:, 0:iw], SILU[:, 0:iw], b[:, iw:2 * iw],
                             OP.mult),
                         waits=[("sc", sl_v)], inc="ve")
            hv_prev = h_v
            gu_free[ct % 3] = [("sc", sl_v), ("ve", h_v)]
            vdrain()
            akey = "A0" if ct % 2 == 0 else "A1"
            qready, _ = quant_tile(HT[:, 0:IW], IW, [("ve", h_v)],
                                   q_waits=q_guard(akey),
                                   qo=QS_q[:, (ct % 2) * IW:(ct % 2 + 1) * IW])
            if dnp is not None:
                emit_adds(*dnp)
            pend_h.append((ct, akey, qready))
        i_off += IW
    for pe in pend_h:
        flush_h(*pe)
    pend_h = []

    ffn_deps = [("ve", p.t("ve"))]
    for ct in range(16):
        ln_tile(Xv[:, ct, :], ffn_deps)
        ffn_deps = []
        p.emit("sync",
               lambda e, c=ct: e.dma_start(out=dout_v[:, c, :], in_=Xv[:, c, :]),
               waits=[("ve", p.t("ve")), ("sc", p.t("sc"))],
               inc="st", inc_n=16)
    st_total = p.t("st")
    p.emit("sync", lambda e: e.wait_ge(p.sems["st"], st_total))

    build_program.counts = dict(p.cnt)
    build_program.nops = {e: len(v) for e, v in p.ops.items()}
    with p.es, nc.Block() as block:
        @block.sync
        def _(sync):
            for op in p.ops["sync"]:
                op(sync)

        @block.gpsimd
        def _(gpsimd):
            for op in p.ops["gpsimd"]:
                op(gpsimd)

        @block.vector
        def _(vector):
            for op in p.ops["vector"]:
                op(vector)

        @block.scalar
        def _(scalar):
            for op in p.ops["scalar"]:
                op(scalar)

        @block.tensor
        def _(tensor):
            for op in p.ops["tensor"]:
                op(tensor)

    return nc


_NC_CACHE = None


def kernel(L0_attn_x, L0_attn_wq, L0_attn_wk, L0_attn_wv, L0_attn_wo,
           L0_ffn_w_gate, L0_ffn_w_up, L0_ffn_w_down):
    global _NC_CACHE
    x = np.ascontiguousarray(np.asarray(L0_attn_x, dtype=np.float32))
    wq = np.ascontiguousarray(np.asarray(L0_attn_wq, dtype=np.float32))
    wk = np.ascontiguousarray(np.asarray(L0_attn_wk, dtype=np.float32))
    wv = np.ascontiguousarray(np.asarray(L0_attn_wv, dtype=np.float32))
    wo = np.ascontiguousarray(np.asarray(L0_attn_wo, dtype=np.float32))
    wg = np.ascontiguousarray(np.asarray(L0_ffn_w_gate, dtype=np.float32))
    wu = np.ascontiguousarray(np.asarray(L0_ffn_w_up, dtype=np.float32))
    wd = np.ascontiguousarray(np.asarray(L0_ffn_w_down, dtype=np.float32))

    if _NC_CACHE is None:
        _NC_CACHE = build_program()
    nc = _NC_CACHE

    in_maps = []
    for c in range(8):
        b, h = c // 2, c % 2
        own = x[b, h * T:(h + 1) * T, :]
        pre = x[b, 0:T, :]
        fl = np.full((128, 1), float(h), dtype=np.float32)
        in_maps.append({
            "x_own": np.ascontiguousarray(own),
            "x_pre": np.ascontiguousarray(pre),
            "flag": fl,
            "wq": wq, "wk": wk, "wv": wv, "wo": wo,
            "w_gate": wg, "w_up": wu, "w_down": wd,
        })
    res = run_bass_kernel_spmd(nc, in_maps, core_ids=list(range(8)))
    out = np.empty((Bq, Sq, D), dtype=np.float32)
    for c in range(8):
        b, h = c // 2, c % 2
        out[b, h * T:(h + 1) * T, :] = np.asarray(res.results[c]["out"],
                                                  dtype=np.float32)
    return out
